# version 11
# baseline (speedup 1.0000x reference)
import numpy as np
import ml_dtypes
import concourse.bass as bass
import concourse.mybir as mybir
from concourse.bass_utils import run_bass_kernel_spmd
from contextlib import ExitStack

F32 = mybir.dt.float32
BF16 = mybir.dt.bfloat16
AF = mybir.ActivationFunctionType
ALU = mybir.AluOpType
AX = mybir.AxisListType.X

N_CORES = 8
D = 1024
S = 8192
B = 2
T = 2048
NC = 8
H = 16
DH = 64
HP = 8
QB = 512
NQB = 4
EPS = 1e-5
CLAMP_MAX = float(np.log(1.0 / 0.01))
BF = ml_dtypes.bfloat16
ENGINES = ("tensor", "vector", "scalar", "gpsimd", "sync")


class Sch:
    def __init__(self, nc, stack):
        self.nc = nc
        self.steps = {e: [] for e in ENGINES}
        self.sem = {}
        self.cnt = {}
        for e in ENGINES:
            self.sem[e] = stack.enter_context(nc.semaphore(f"c_{e}"))
            self.cnt[e] = 0
        for e in ("sync", "gpsimd"):
            self.sem["d" + e] = stack.enter_context(nc.semaphore(f"d_{e}"))
            self.cnt["d" + e] = 0
        self.waited = {e: {} for e in ENGINES}

    def do(self, eng, fn, deps=(), mark=True, fence=False):
        deps = tuple(d for d in deps if d is not None)
        if fence:
            deps = deps + ((self.sem[eng], self.cnt[eng], True),)
        inc = None
        if mark:
            self.cnt[eng] += 1
            inc = (self.sem[eng], 1)
        self.steps[eng].append((deps, fn, inc))
        return (self.sem[eng], self.cnt[eng])

    def dma(self, eng, fn, deps=()):
        key = "d" + eng
        self.cnt[key] += 16
        inc = (self.sem[key], 16)
        deps = tuple(d for d in deps if d is not None)
        self.steps[eng].append((deps, fn, inc))
        return (self.sem[key], self.cnt[key])

    def flush(self):
        nc = self.nc
        if not any(self.steps[e] for e in ENGINES):
            return
        with nc.Block() as block:
            for eng in ENGINES:
                steps = self.steps[eng]
                if not steps:
                    continue

                def body(h, eng=eng, steps=steps):
                    waited = self.waited[eng]
                    own = self.sem[eng]
                    for deps, fn, inc in steps:
                        for dep in deps:
                            sem, val = dep[0], dep[1]
                            force = len(dep) > 2 and dep[2]
                            if sem is own and not force:
                                continue
                            k = sem.name
                            if waited.get(k, 0) < val:
                                h.wait_ge(sem, val)
                                waited[k] = val
                        ins = fn(h)
                        if inc is not None:
                            ins.then_inc(inc[0], inc[1])

                getattr(block, eng)(body)
        self.steps = {e: [] for e in ENGINES}

    def final_wait(self):
        for q in ("sync", "gpsimd"):
            if self.cnt["d" + q]:
                self.do(q, lambda h: h.nop(),
                        deps=[(self.sem["d" + q], self.cnt["d" + q])], mark=False)
        self.flush()

    def barrier(self):
        snap = {e: self.cnt[e] for e in ENGINES}
        dsnap = {q: self.cnt["d" + q] for q in ("sync", "gpsimd")}
        for e in ENGINES:
            deps = [(self.sem[o], snap[o]) for o in ENGINES
                    if o != e and snap[o] > 0]
            deps += [(self.sem["d" + q], dsnap[q]) for q in ("sync", "gpsimd")
                     if dsnap[q] > 0]
            self.do(e, lambda h: h.nop(), deps=deps, mark=False)



def emit_ln(s, C, x, xn, t0, n, prev, x_ev=None):
    acc, accb, sqv, srow = C["ln_acc"], C["ln_accb"], C["ln_sqv"], C["ln_srow"]
    ln_ps, ln_bc = C["ln_ps"], C["ln_bc"]
    ones_col, ones_row = C["ones_col"], C["ones_row"]

    def cs(c):
        return x[:, c * T + t0: c * T + t0 + n]

    deps0 = [x_ev, prev.get("stats_mm")]
    s.do("vector", lambda h: h.tensor_add(acc[:, 0:n], cs(0), cs(1)), deps=deps0, mark=False)
    for c in range(2, NC):
        s.do("vector", (lambda c: lambda h: h.tensor_add(acc[:, 0:n], acc[:, 0:n], cs(c)))(c),
             mark=False)
    s.do("vector", lambda h: h.tensor_copy(accb[:, 0:n], acc[:, 0:n]), mark=False)
    s.do("vector", lambda h: h.tensor_mul(acc[:, 0:n], cs(0), cs(0)), mark=False)
    for c in range(1, NC):
        s.do("vector", (lambda c: lambda h: h.tensor_mul(sqv[:, 0:n], cs(c), cs(c)))(c),
             mark=False)
        s.do("vector", lambda h: h.tensor_add(acc[:, 0:n], acc[:, 0:n], sqv[:, 0:n]), mark=False)
    ev_v = s.do("vector", lambda h: h.tensor_copy(accb[:, 512:512 + n], acc[:, 0:n]))

    s.do("tensor", lambda h: h.matmul(ln_ps[0:1, 0:n], ones_col[:, 0:1], accb[:, 0:n],
                                      start=True, stop=True), deps=[ev_v], mark=False)
    ev_mm = s.do("tensor", lambda h: h.matmul(ln_ps[0:1, 512:512 + n], ones_col[:, 0:1],
                                              accb[:, 512:512 + n], start=True, stop=True))

    r_rstd = srow[0:1, 0:n]
    r_m = srow[0:1, 512:512 + n]
    r_t = srow[0:1, 1024:1024 + n]
    s.do("vector", lambda h: h.tensor_scalar(r_m, ln_ps[0:1, 0:n], 1.0 / D, None, ALU.mult),
         deps=[ev_mm], mark=False)
    s.do("vector", lambda h: h.tensor_mul(r_t, r_m, r_m), mark=False)
    s.do("vector", lambda h: h.scalar_tensor_tensor(r_t, ln_ps[0:1, 512:512 + n], 1.0 / D,
                                                    r_t, ALU.mult, ALU.subtract), mark=False)
    ev1 = s.do("vector", lambda h: h.tensor_scalar(r_t, r_t, EPS, None, ALU.add))
    s.do("scalar", lambda h: h.activation(r_rstd, r_t, AF.Ln),
         deps=[ev1, prev.get("bc")], mark=False)
    ev2 = s.do("scalar", lambda h: h.activation(r_rstd, r_rstd, AF.Exp, scale=-0.5))
    ev3 = s.do("vector", lambda h: h.tensor_mul(r_m, r_m, r_rstd), deps=[ev2])
    s.do("tensor", lambda h: h.matmul(ln_bc[:, 0:n], ones_row[0:1, :], r_rstd,
                                      start=True, stop=True), deps=[ev3], mark=False)
    ev4 = s.do("tensor", lambda h: h.matmul(ln_bc[:, 512:512 + n], ones_row[0:1, :], r_m,
                                            start=True, stop=True))
    ev_xn = None
    for c in range(NC):
        s.do("vector", (lambda c: lambda h: h.tensor_mul(sqv[:, 0:n], cs(c), ln_bc[:, 0:n]))(c),
             deps=[ev4] if c == 0 else (), mark=False)
        ev_xn = s.do("vector",
                     (lambda c: lambda h: h.tensor_tensor(
                         xn[:, c * T + t0: c * T + t0 + n], sqv[:, 0:n],
                         ln_bc[:, 512:512 + n], ALU.subtract))(c),
                     mark=(c == NC - 1))
    return {"stats_mm": ev_mm, "bc": ev4, "xn": ev_xn}


def emit_ln_all(s, C, x, xn, x_ev=None, prev=None, xn_war=None):
    prev = dict(prev or {})
    if xn_war is not None:
        prev["stats_mm"] = xn_war
    evs = []
    for qb in range(NQB):
        prev = emit_ln(s, C, x, xn, qb * QB, QB, prev, x_ev=x_ev if qb == 0 else None)
        evs.append(prev["xn"])
    return evs, prev



def emit_attention_hp(s, C, hp, qb_specs, scale, kside, qown, vbuf, aug, masks_h,
                      zcat, ev_kv, carry, biasf=None):
    sc = (C["sc0"], C["sc1"])
    pv, smp = C["pv"], C["smp"]
    bc = C["sc0"]
    pbuf = C["pbuf"]
    recipb, rb_sb = C["recipb"], C["rb_sb"]
    ones_all = C["ones_all"]
    mtile = C.get("mtile")

    flat = []
    for qi, (t0, kts) in enumerate(qb_specs):
        for i, (kt, midx) in enumerate(kts):
            flat.append((qi, i, len(kts), t0, kt, midx))
    masked = [fi for fi, f in enumerate(flat) if f[5] is not None]
    mask_dma_ev = {}
    mask_tt_ev = {}

    def emit_mask_dma(ordinal):
        if masks_h is None:
            return
        slot = ordinal % 3
        dep = mask_tt_ev.get(ordinal - 3)
        assert ordinal < 3 or (ordinal - 3) in mask_tt_ev, "mask WAR dep not yet emitted"
        mask_dma_ev[ordinal] = s.dma(
            "sync",
            (lambda ordinal, slot: lambda h: h.dma_start(
                out=mtile[:, slot * 512:(slot + 1) * 512],
                in_=masks_h[ordinal, :, :]))(ordinal, slot),
            deps=[dep])

    n_prefetch = 0
    ev_score = {}
    ev_pr = {}

    def emit_pv_sums(fi):
        qi, i, nkt, t0, kt, midx = flat[fi]
        p = fi % 2
        first = i == 0
        lastk = i == nkt - 1
        deps_pv = [ev_pr[fi], ev_kv[1]]
        if first:
            deps_pv += [carry.get("zn0"), carry.get("zn1")]
        s.do("tensor",
             (lambda kt, p, first, lastk: lambda h: h.matmul(
                 pv[0:64, 0:512], vbuf[:, kt * 128: kt * 128 + 64],
                 pbuf[:, p * 1024: p * 1024 + 512],
                 start=first, stop=lastk, tile_position=(0, 0)))(kt, p, first, lastk),
             deps=deps_pv, mark=lastk)
        if lastk:
            carry["pv0_last"] = (s.sem["tensor"], s.cnt["tensor"])
        ev_pvl = s.do("tensor",
                      (lambda kt, p, first, lastk: lambda h: h.matmul(
                          pv[64:128, 512:1024], vbuf[:, kt * 128 + 64: kt * 128 + 128],
                          pbuf[:, p * 1024 + 512: p * 1024 + 1024],
                          start=first, stop=lastk, tile_position=(0, 64)))(kt, p, first, lastk),
                      mark=True)
        if lastk:
            carry["pv1_last"] = ev_pvl
        deps_sm = [ev_kv[2]]
        if first:
            deps_sm += [carry.get("rc0"), carry.get("rc1")]
        s.do("tensor",
             (lambda kt, p, first, lastk: lambda h: h.matmul(
                 smp[0:1, 0:512], aug(kt, 0),
                 pbuf[:, p * 1024: p * 1024 + 512],
                 start=first, stop=lastk, tile_position=(0, 0)))(kt, p, first, lastk),
             deps=deps_sm, mark=lastk)
        if lastk:
            carry["sm0_last"] = (s.sem["tensor"], s.cnt["tensor"])
        ev_sml = s.do("tensor",
                      (lambda kt, p, first, lastk: lambda h: h.matmul(
                          smp[64:65, 512:1024], aug(kt, 1),
                          pbuf[:, p * 1024 + 512: p * 1024 + 1024],
                          start=first, stop=lastk, tile_position=(0, 64)))(kt, p, first, lastk),
                      mark=True)
        if lastk:
            carry["sm1_last"] = ev_sml
        carry["pbw%d" % p] = ev_sml

    def emit_normalize(qi, t0):
        ev_rc0 = s.do("vector", lambda h: h.reciprocal(recipb[0:1, 0:512], smp[0:1, 0:512]),
                      deps=[carry["sm0_last"], carry.get("bc0")])
        ev_rc1 = s.do("vector", lambda h: h.reciprocal(recipb[64:65, 0:512],
                                                       smp[64:65, 512:1024]),
                      deps=[carry["sm1_last"], carry.get("bc1")])
        ev_bc0 = s.do("tensor", lambda h: h.matmul(bc[0:64, 0:512], ones_all[0:1, 0:64],
                                                   recipb[0:1, 0:512], start=True, stop=True,
                                                   tile_position=(0, 0)),
                      deps=[ev_rc0, carry.get("rb0"),
                            carry.get("expw0"), carry.get("expw1")])
        ev_bc1 = s.do("tensor", lambda h: h.matmul(bc[64:128, 512:1024], ones_all[64:65, 0:64],
                                                   recipb[64:65, 0:512], start=True, stop=True,
                                                   tile_position=(64, 64)),
                      deps=[ev_rc1, carry.get("rb1")])
        ev_rb0 = s.do("scalar", lambda h: h.copy(rb_sb[0:64, 0:512], bc[0:64, 0:512]),
                      deps=[ev_bc0, carry.get("zn0")])
        ev_rb1 = s.do("scalar", lambda h: h.copy(rb_sb[64:128, 512:1024], bc[64:128, 512:1024]),
                      deps=[ev_bc1, carry.get("zn1")])
        ev_zn0 = s.do("vector",
                      (lambda hp, t0: lambda h: h.tensor_mul(
                          zcat[0:64, hp * T + t0: hp * T + t0 + QB],
                          pv[0:64, 0:512], rb_sb[0:64, 0:512]))(hp, t0),
                      deps=[ev_rb0, carry["pv0_last"]])
        ev_zn1 = s.do("vector",
                      (lambda hp, t0: lambda h: h.tensor_mul(
                          zcat[64:128, hp * T + t0: hp * T + t0 + QB],
                          pv[64:128, 512:1024], rb_sb[64:128, 512:1024]))(hp, t0),
                      deps=[ev_rb1, carry["pv1_last"]])
        carry.update(rc0=ev_rc0, rc1=ev_rc1, bc0=ev_bc0, bc1=ev_bc1,
                     rb0=ev_rb0, rb1=ev_rb1, zn0=ev_zn0, zn1=ev_zn1)

    def lagged(fi):
        if fi < 0:
            return
        emit_pv_sums(fi)
        f = flat[fi]
        if f[1] == f[2] - 1:
            emit_normalize(f[0], f[3])
            carry["rb_pending"] = True

    for fi, (qi, i, nkt, t0, kt, midx) in enumerate(flat):
        p = fi % 2
        while n_prefetch < len(masked) and masked[n_prefetch] <= fi + 2:
            emit_mask_dma(n_prefetch)
            n_prefetch += 1
        deps_s = [carry.get("expw%d" % p)]
        if p == 0 and carry.get("rb_pending"):
            deps_s += [carry.get("rb0"), carry.get("rb1")]
            carry["rb_pending"] = False
        if fi == 0:
            deps_s += [ev_kv[0], carry.get("qown_ev")]
        s.do("tensor",
             (lambda kt, p, t0: lambda h: h.matmul(
                 sc[p][:, 0:512], kside[0:64, kt * 128:(kt + 1) * 128],
                 qown[0:64, hp * T + t0: hp * T + t0 + QB],
                 start=True, stop=True, tile_position=(0, 0)))(kt, p, t0),
             deps=deps_s, mark=False)
        ev_s = s.do("tensor",
                    (lambda kt, p, t0: lambda h: h.matmul(
                        sc[p][:, 512:1024], kside[64:128, kt * 128:(kt + 1) * 128],
                        qown[64:128, hp * T + t0: hp * T + t0 + QB],
                        start=True, stop=True, tile_position=(64, 0)))(kt, p, t0))
        lagged(fi - 2)
        if biasf is None:
            ev_e = s.do("scalar",
                        (lambda p, scale: lambda h: h.activation(
                            pbuf[:, p * 1024:(p + 1) * 1024], sc[p][:, 0:1024],
                            AF.Exp, scale=scale))(p, scale),
                        deps=[ev_s, carry.get("pbw%d" % p)])
        else:
            s.do("scalar",
                 (lambda p, scale, kt: lambda h: h.activation(
                     pbuf[:, p * 1024: p * 1024 + 512], sc[p][:, 0:512],
                     AF.Exp, scale=scale, bias=biasf(kt, 0)))(p, scale, kt),
                 deps=[ev_s, carry.get("pbw%d" % p)], mark=False)
            ev_e = s.do("scalar",
                        (lambda p, scale, kt: lambda h: h.activation(
                            pbuf[:, p * 1024 + 512: p * 1024 + 1024], sc[p][:, 512:1024],
                            AF.Exp, scale=scale, bias=biasf(kt, 1)))(p, scale, kt))
        carry["expw%d" % p] = ev_e
        ev_p = ev_e
        if midx is not None:
            ordinal = masked.index(fi)
            mslot = ordinal % 3
            s.do("vector",
                 (lambda p, mslot: lambda h: h.tensor_mul(
                     pbuf[:, p * 1024: p * 1024 + 512],
                     pbuf[:, p * 1024: p * 1024 + 512],
                     mtile[:, mslot * 512:(mslot + 1) * 512]))(p, mslot),
                 deps=[ev_e, mask_dma_ev[ordinal]], mark=False)
            ev_p = s.do("vector",
                        (lambda p, mslot: lambda h: h.tensor_mul(
                            pbuf[:, p * 1024 + 512: p * 1024 + 1024],
                            pbuf[:, p * 1024 + 512: p * 1024 + 1024],
                            mtile[:, mslot * 512:(mslot + 1) * 512]))(p, mslot))
            mask_tt_ev[ordinal] = ev_p
        ev_pr[fi] = ev_p
        ev_score[fi] = ev_s
    lagged(len(flat) - 2)
    lagged(len(flat) - 1)
    return {"score": ev_score[len(flat) - 1], "pv": carry["pv1_last"],
            "sum": carry["sm1_last"], "znlast": carry["zn1"]}



def emit_proj_residual(s, C, wsb, zsrc, x, out_h, mm_ps, xstage, ev_ready, ev_w, ev_x,
                       bias=None, wcols=D, n_k=NC):
    prev_store = [None, None]
    evs = []
    for oc in range(NC):
        for qb in range(NQB):
            p = (oc * NQB + qb) % 2
            t0 = qb * QB
            ev_mm = None
            for dc in range(n_k):
                deps = []
                if dc == 0:
                    deps = [ev_w, prev_store[p]]
                    if oc == 0 and qb == 0:
                        deps.append(ev_ready)
                        deps.append(ev_x)
                ev_mm = s.do("tensor",
                             (lambda oc, dc, t0, p: lambda h: h.matmul(
                                 mm_ps[p][:, :],
                                 wsb[:, dc * wcols + oc * 128: dc * wcols + oc * 128 + 128],
                                 zsrc[:, dc * T + t0: dc * T + t0 + QB],
                                 start=(dc == 0), stop=(dc == n_k - 1)))(oc, dc, t0, p),
                             deps=deps, mark=(dc == n_k - 1))
            if out_h is not None:
                ev_a = s.do("vector",
                            (lambda p, oc, t0: lambda h: h.tensor_add(
                                xstage[:, p * 512:(p + 1) * 512], mm_ps[p][:, :],
                                x[:, oc * T + t0: oc * T + t0 + QB]))(p, oc, t0),
                            deps=[ev_mm])
                ev_st = s.dma("sync",
                              (lambda p, oc, t0: lambda h: h.dma_start(
                                  out=out_h[oc * 128:(oc + 1) * 128, t0:t0 + QB],
                                  in_=xstage[:, p * 512:(p + 1) * 512]))(p, oc, t0),
                              deps=[ev_a])
                prev_store[p] = ev_st
                evs.append(ev_a)
            else:
                ev_a = s.do("vector",
                            (lambda p, oc, t0: lambda h: h.tensor_add(
                                x[:, oc * T + t0: oc * T + t0 + QB], mm_ps[p][:, :],
                                x[:, oc * T + t0: oc * T + t0 + QB]))(p, oc, t0),
                            deps=[ev_mm])
                prev_store[p] = ev_a
                evs.append(ev_a)
    return evs


def ln_scratch(stack, nc, sb, psf):
    C = {}
    C["ln_acc"] = sb("lnacc", [128, 512], F32)
    C["ln_accb"] = sb("lnaccb", [128, 1024], BF16)
    C["ln_sqv"] = sb("lnsqv", [128, 512], F32)
    C["ln_srow"] = sb("lnsrow", [1, 2048], F32)
    C["ln_ps"] = psf("lnps", [128, 1024])
    C["ln_bc"] = C["ln_ps"]
    return C



def build_p1():
    nc = bass.Bass("TRN2", target_bir_lowering=False)
    x_h = nc.dram_tensor("x", [D, T], F32, kind="ExternalInput")
    wqT_h = nc.dram_tensor("wqT", [D, D], F32, kind="ExternalInput")
    wkT_h = nc.dram_tensor("wkT", [D, D], F32, kind="ExternalInput")
    wvT_h = nc.dram_tensor("wvT", [D, D], F32, kind="ExternalInput")
    bq_h = nc.dram_tensor("bq", [128, NC], F32, kind="ExternalInput")
    bk_h = nc.dram_tensor("bk", [128, NC], F32, kind="ExternalInput")
    bv_h = nc.dram_tensor("bv_row", [1, D], F32, kind="ExternalInput")
    onesc_h = nc.dram_tensor("ones_col", [128, 1], BF16, kind="ExternalInput")
    onesr_h = nc.dram_tensor("ones_row", [1, 128], F32, kind="ExternalInput")
    qfm_h = nc.dram_tensor("qfm", [D, T], BF16, kind="ExternalOutput")
    kfm_h = nc.dram_tensor("kfm", [D, T], BF16, kind="ExternalOutput")
    vtm_h = nc.dram_tensor("vtm", [T, D], BF16, kind="ExternalOutput")

    with ExitStack() as stack:
        sb = lambda n, sh, dt: stack.enter_context(nc.sbuf_tensor("s_" + n, sh, dt))
        psf = lambda n, sh: stack.enter_context(nc.psum_tensor("q_" + n, sh, F32))
        s = Sch(nc, stack)

        x = sb("x", [128, NC * T], F32)
        xn = sb("xn", [128, NC * T], BF16)
        wq = sb("wq", [128, NC * D], BF16)
        wk = sb("wk", [128, NC * D], BF16)
        wv = sb("wv", [128, NC * D], BF16)
        bq = sb("bq", [128, NC], F32)
        bk = sb("bk", [128, NC], F32)
        bvb = sb("bvb", [128, D], F32)
        bv_r = sb("bvr", [1, D], F32)
        stage = sb("stage", [128, 4 * 512], BF16)
        vstage = sb("vstage", [128, 2 * D], BF16)
        C = ln_scratch(stack, nc, sb, psf)
        C["ones_col"] = sb("onesc", [128, 1], BF16)
        C["ones_row"] = sb("onesr", [1, 128], F32)
        mm_ps = [psf("mm0", [128, 512]), psf("mm1", [128, 512])]
        v_ps = [psf("v0", [128, 1024]), psf("v1", [128, 1024])]

        ev_x = s.dma("sync", lambda h: h.dma_start(
            out=x[:, :].rearrange("p (c x) -> p c x", c=8), in_=x_h.ap().rearrange("(c p) t -> p c t", p=128)))
        ev_wq = s.dma("gpsimd", lambda h: h.dma_start(
            out=wq[:, :].rearrange("p (c x) -> p c x", c=8), in_=wqT_h.ap().rearrange("(c p) o -> p c o", p=128)))
        ev_wk = s.dma("gpsimd", lambda h: h.dma_start(
            out=wk[:, :].rearrange("p (c x) -> p c x", c=8), in_=wkT_h.ap().rearrange("(c p) o -> p c o", p=128)))
        ev_wv = s.dma("gpsimd", lambda h: h.dma_start(
            out=wv[:, :].rearrange("p (c x) -> p c x", c=8), in_=wvT_h.ap().rearrange("(c p) o -> p c o", p=128)))
        s.dma("sync", lambda h: h.dma_start(out=bq[:, :], in_=bq_h[:, :]))
        ev_b = s.dma("sync", lambda h: h.dma_start(out=bk[:, :], in_=bk_h[:, :]))
        ev_bv = s.dma("sync", lambda h: h.dma_start(out=bv_r[:, :], in_=bv_h[:, :]))
        ev_c = s.dma("sync", lambda h: h.dma_start(out=C["ones_col"][:, :], in_=onesc_h[:, :]))
        ev_r = s.dma("sync", lambda h: h.dma_start(out=C["ones_row"][:, :], in_=onesr_h[:, :]))

        s.do("tensor", lambda h: h.matmul(v_ps[0][:, 0:512], C["ones_row"][0:1, :],
                                          bv_r[0:1, 0:512], start=True, stop=True),
             deps=[ev_bv, ev_r], mark=False)
        ev_bvb = s.do("tensor", lambda h: h.matmul(v_ps[0][:, 512:1024], C["ones_row"][0:1, :],
                                                   bv_r[0:1, 512:1024], start=True, stop=True))
        ev_bvb = s.do("scalar", lambda h: h.copy(bvb[:, :], v_ps[0][:, :]), deps=[ev_bvb])

        ln_evs, ln_prev = emit_ln_all(s, C, x, xn, ev_x)
        s.flush()

        prev_evac = [None, None]
        prev_store = [None, None, None, None]
        slot = 0
        for wsb, bias, out_h, wev in ((wq, bq, qfm_h, ev_wq), (wk, bk, kfm_h, ev_wk)):
            for oc in range(NC):
                for qb in range(NQB):
                    p = slot % 2
                    t0 = qb * QB
                    ev_mm = None
                    for dc in range(NC):
                        deps = []
                        if dc == 0:
                            deps = [ln_evs[qb], wev, prev_evac[p]]
                        ev_mm = s.do("tensor",
                                     (lambda wsb, oc, dc, t0, p: lambda h: h.matmul(
                                         mm_ps[p][:, :],
                                         wsb[:, dc * D + oc * 128: dc * D + oc * 128 + 128],
                                         xn[:, dc * T + t0: dc * T + t0 + QB],
                                         start=(dc == 0), stop=(dc == NC - 1)))(wsb, oc, dc, t0, p),
                                     deps=deps, mark=(dc == NC - 1))
                    ev_e = s.do("scalar",
                                (lambda p, oc, slot, bias: lambda h: h.activation(
                                    stage[:, (slot % 4) * 512:((slot % 4) + 1) * 512],
                                    mm_ps[p][:, :], AF.Identity, bias=bias[:, oc:oc + 1]))(p, oc, slot, bias),
                                deps=[ev_mm, ev_b, prev_store[slot % 4]])
                    prev_evac[p] = ev_e
                    ev_d = s.dma("sync",
                                 (lambda slot, oc, t0, out_h: lambda h: h.dma_start(
                                     out=out_h[oc * 128:(oc + 1) * 128, t0:t0 + QB],
                                     in_=stage[:, (slot % 4) * 512:((slot % 4) + 1) * 512]))(slot, oc, t0, out_h),
                                 deps=[ev_e])
                    prev_store[slot % 4] = ev_d
                    slot += 1
        s.flush()

        prev_add = [None, None]
        for tc in range(T // 128):
            p = tc % 2
            qb = (tc * 128) // QB
            ev_mm = None
            for half in range(2):
                for dc in range(NC):
                    deps = []
                    if half == 0 and dc == 0:
                        deps = [ln_evs[qb], ev_wv, prev_add[p], ev_bvb]
                    ev_mm = s.do("tensor",
                                 (lambda tc, dc, half, p: lambda h: h.matmul(
                                     v_ps[p][:, half * 512:(half + 1) * 512],
                                     xn[:, dc * T + tc * 128: dc * T + (tc + 1) * 128],
                                     wv[:, dc * D + half * 512: dc * D + (half + 1) * 512],
                                     start=(dc == 0), stop=(dc == NC - 1)))(tc, dc, half, p),
                                 deps=deps, mark=(dc == NC - 1 and half == 1))
            ev_a = s.do("vector",
                        (lambda p: lambda h: h.tensor_add(
                            vstage[:, p * D:(p + 1) * D], v_ps[p][:, :], bvb[:, :]))(p),
                        deps=[ev_mm])
            prev_add[p] = ev_a
            s.dma("sync",
                  (lambda tc, p: lambda h: h.dma_start(
                      out=vtm_h[tc * 128:(tc + 1) * 128, :],
                      in_=vstage[:, p * D:(p + 1) * D]))(tc, p),
                  deps=[ev_a])
        s.final_wait()
    return nc



CAUSAL_B = [28, 32, 60, 64]
CAUSAL_G0 = [0, 4, 32, 36]
NMASK = sum(CAUSAL_B) - sum(CAUSAL_G0)


def causal_qb_specs():
    specs = []
    ordinal = 0
    for s in range(NQB):
        kts = []
        for g in range(CAUSAL_B[s]):
            if g >= CAUSAL_G0[s]:
                kts.append((g, ordinal))
                ordinal += 1
            else:
                kts.append((g, None))
        specs.append((s * QB, kts))
    assert ordinal == NMASK
    return specs


def build_p2(dbg=False, merged=False):
    nc = bass.Bass("TRN2", target_bir_lowering=False)
    x_h = nc.dram_tensor("x", [D, T], F32, kind="ExternalInput")
    qown_h = nc.dram_tensor("qown", [D, T], BF16, kind="ExternalInput")
    kfull_h = nc.dram_tensor("kfull", [D, S], BF16, kind="ExternalInput")
    vall_h = nc.dram_tensor("vall", [HP, 128, S], BF16, kind="ExternalInput")
    proj_h = nc.dram_tensor("proj", [D, D], F32, kind="ExternalInput")
    masks_h = nc.dram_tensor("masks_all", [NMASK, 128, 512], BF16, kind="ExternalInput")
    onesall_h = nc.dram_tensor("ones_all", [128, 128], F32, kind="ExternalInput")
    aug8_h = nc.dram_tensor("aug8", [128, 1], BF16, kind="ExternalInput")
    if merged:
        hh3 = p3_handles(nc, with_x=False)
        x1_h = None
    else:
        x1_h = nc.dram_tensor("x1", [D, T], F32, kind="ExternalOutput")
    if dbg:
        dz_h = nc.dram_tensor("d_zcat", [128, HP * T], BF16, kind="ExternalOutput")

    with ExitStack() as stack:
        sb = lambda n, sh, dt: stack.enter_context(nc.sbuf_tensor("s_" + n, sh, dt))
        psf = lambda n, sh: stack.enter_context(nc.psum_tensor("q_" + n, sh, F32))
        s = Sch(nc, stack)

        zcat = sb("zcat", [128, HP * T], BF16)
        C = {}
        C["ones_all"] = sb("onesall", [128, 128], F32)
        aug8 = sb("aug8", [128, 1], BF16)
        C["recipb"] = sb("recipb", [128, 1024], F32)
        C["rb_sb"] = sb("rb", [128, 1024], BF16)
        x_outer = sb("x", [128, NC * T], F32) if merged else None

        s.dma("sync", lambda h: h.dma_start(out=C["ones_all"][:, :], in_=onesall_h[:, :]))
        ev_cst = s.dma("sync", lambda h: h.dma_start(out=aug8[:, :], in_=aug8_h[:, :]))

        with ExitStack() as ph:
            sbp = lambda n, sh, dt: ph.enter_context(nc.sbuf_tensor("sp_" + n, sh, dt))
            psp = lambda n, sh: ph.enter_context(nc.psum_tensor("qp_" + n, sh, F32))
            qown = sbp("qown", [128, HP * T], BF16)
            kside = sbp("kside", [128, S], BF16)
            vbuf = sbp("vbuf", [128, S], BF16)
            C["pbuf"] = sbp("pbuf", [128, 2048], BF16)
            C["mtile"] = sbp("mtile", [128, 3 * 512], BF16)
            C["sc0"] = psp("sc0", [128, 1024])
            C["sc1"] = psp("sc1", [128, 1024])
            C["pv"] = psp("pv", [128, 1024])
            C["smp"] = psp("smp", [128, 1024])

            ev_q = s.dma("sync", lambda h: h.dma_start(
                out=qown[:, :].rearrange("p (c x) -> p c x", c=8),
                in_=qown_h.ap().rearrange("(c p) t -> p c t", p=128)))

            specs = causal_qb_specs()
            carry = {"qown_ev": ev_q}
            last = None
            for hp in range(HP):
                deps_k = [] if hp == 0 else [last["score"]]
                deps_v = [] if hp == 0 else [last["pv"]]
                ev_k = s.dma("sync",
                             (lambda hp: lambda h: h.dma_start(
                                 out=kside[:, :], in_=kfull_h[hp * 128:(hp + 1) * 128, :]))(hp),
                             deps=deps_k)
                ev_v = s.dma("sync",
                             (lambda hp: lambda h: h.dma_start(
                                 out=vbuf[:, :], in_=vall_h[hp, :, :]))(hp),
                             deps=deps_v)
                last = emit_attention_hp(
                    s, C, hp, specs, 0.125, kside, qown, vbuf,
                    aug=lambda kt, hh: aug8[:, 0:1], masks_h=masks_h, zcat=zcat,
                    ev_kv=(ev_k, ev_v, ev_cst), carry=carry)
            ev_attn_done = last["znlast"]
            if dbg:
                s.dma("sync", lambda h: h.dma_start(out=dz_h[:, :], in_=zcat[:, :]),
                      deps=[ev_attn_done])
            s.flush()

        with ExitStack() as ph:
            sbp = lambda n, sh, dt: ph.enter_context(nc.sbuf_tensor("sp_" + n, sh, dt))
            x = x_outer if merged else sbp("x", [128, NC * T], F32)
            pw = sbp("pw", [128, NC * D], BF16)
            xstage = sbp("xstage", [128, 2 * 512], F32)
            psp = lambda n, sh: ph.enter_context(nc.psum_tensor("qp_" + n, sh, F32))
            mm_ps = [psp("pm0", [128, 512]), psp("pm1", [128, 512])]

            ev_x = s.dma("sync", lambda h: h.dma_start(
                out=x[:, :].rearrange("p (c x) -> p c x", c=8),
                in_=x_h.ap().rearrange("(c p) t -> p c t", p=128)))
            ev_pw = s.dma("gpsimd", lambda h: h.dma_start(
                out=pw[:, :].rearrange("p (c x) -> p c x", c=8),
                in_=proj_h.ap().rearrange("(c p) o -> p c o", p=128)))

            emit_proj_residual(s, C, pw, zcat, x, x1_h, mm_ps, xstage,
                               ev_attn_done, ev_pw, ev_x)
            s.flush()
        if merged:
            s.barrier()
            s.flush()
            emit_p3_body(nc, stack, s, hh3, x=x_outer)
        s.final_wait()
    return nc



def emit_p3_body(nc, stack, s, hh, x=None, dbg=False):
    (x_h, wT_h, wpT_h, bp_h, sim_h, temp_h, wdT_h, wuT_h, wkT_h, onesc_h,
     onesr_h, seg_h, ident_h, x2_h, q1fm_h, q1tm_h, expsq_h) = (
        hh.get("x"), hh["wT"], hh["wpT"], hh["bp"], hh["sim"], hh["temp"],
        hh["wdT"], hh["wuT"], hh["wk1T"], hh["ones_col"], hh["ones_row"],
        hh["segmat4"], hh["ident_bf"], hh["x2"], hh["q1fm"], hh["q1tm"], hh["nb"])
    if dbg:
        dppre_h, drs_h, dretT_h, dtsc_h, dscal_h, dsimn_h, dxn_h = (
            hh["d_ppre"], hh["d_rs"], hh["d_retT"], hh["d_tsc"], hh["d_scal"],
            hh["d_simn"], hh["d_xn"])
    sb = lambda n, sh, dt: stack.enter_context(nc.sbuf_tensor("s3_" + n, sh, dt))
    psf = lambda n, sh: stack.enter_context(nc.psum_tensor("q3_" + n, sh, F32))
    xn = sb("xn", [128, NC * T], BF16)
    if x is None:
        x = sb("x", [128, NC * T], F32)
        ev_x = s.dma("sync", lambda h: h.dma_start(
            out=x[:, :].rearrange("p (c x) -> p c x", c=8),
            in_=x_h.ap().rearrange("(c p) t -> p c t", p=128)))
    else:
        ev_x = None
    C = ln_scratch(stack, nc, sb, psf)
    C["ones_col"] = sb("onesc", [128, 1], BF16)
    C["ones_row"] = sb("onesr", [1, 128], F32)
    ident = sb("ident", [128, 128], BF16)

    s.dma("sync", lambda h: h.dma_start(out=C["ones_col"][:, :], in_=onesc_h[:, :]))
    ev_c = s.dma("sync", lambda h: h.dma_start(out=C["ones_row"][:, :], in_=onesr_h[:, :]))
    ev_id = s.dma("sync", lambda h: h.dma_start(out=ident[:, :], in_=ident_h[:, :]))

    ln_evs, ln_prev = emit_ln_all(s, C, x, xn, ev_x)
    s.flush()

    with ExitStack() as ph:
        sbp = lambda n, sh, dt: ph.enter_context(nc.sbuf_tensor("sp_" + n, sh, dt))
        wt = sbp("wt", [128, NC * D], BF16)
        wpT = sbp("wpT", [128, NC * 256], BF16)
        bp = sbp("bp", [128, 2], F32)
        sim = sbp("sim", [128, 2 * 4], F32)
        simn = sbp("simn", [128, 2 * 4], BF16)
        wdT = sbp("wdT", [128, NC * 15], BF16)
        wuT = sbp("wuT", [15, D], BF16)
        seg = sbp("seg", [4, 15], BF16)
        ppre = sbp("ppre", [128, 2 * T], BF16)
        temp = sbp("temp", [1, 1], F32)
        srow2 = sbp("srow2", [1, 1024], F32)
        scal = sbp("scal", [128, 1], F32)
        rs = sbp("rs", [128, 4 * NQB], F32)
        gt = sbp("gt", [128, 24], F32)
        retsb = sbp("retsb", [128, 4], BF16)
        retT = sbp("retT", [4, T], BF16)
        tvsb = sbp("tvsb", [15, 512], BF16)
        tscsb = sbp("tscsb", [15, 512], BF16)

        psp = lambda n, sh: ph.enter_context(nc.psum_tensor("q_" + n, sh, F32))
        g_ps = psp("gps", [128, 1024])
        g2_ps = psp("g2ps", [128, 512])
        mm_ps = [psp("fm0", [128, 512]), psp("fm1", [128, 512])]

        ev_wt = s.dma("gpsimd", lambda h: h.dma_start(
            out=wt[:, :].rearrange("p (c x) -> p c x", c=8), in_=wT_h.ap().rearrange("(c p) o -> p c o", p=128)))
        ev_wp = s.dma("gpsimd", lambda h: h.dma_start(
            out=wpT[:, :].rearrange("p (c x) -> p c x", c=8), in_=wpT_h.ap().rearrange("(c p) o -> p c o", p=128)))
        ev_wd = s.dma("gpsimd", lambda h: h.dma_start(
            out=wdT[:, :].rearrange("p (c x) -> p c x", c=8), in_=wdT_h.ap().rearrange("(c p) o -> p c o", p=128)))
        ev_wu = s.dma("gpsimd", lambda h: h.dma_start(out=wuT[:, :], in_=wuT_h[:, :]))
        ev_sim = s.dma("gpsimd", lambda h: h.dma_start(
            out=sim[:, :].rearrange("p (c x) -> p c x", c=2), in_=sim_h.ap().rearrange("(c p) e -> p c e", p=128)))
        ev_bp = s.dma("sync", lambda h: h.dma_start(out=bp[:, :], in_=bp_h[:, :]))
        ev_tm = s.dma("sync", lambda h: h.dma_start(out=temp[:, :], in_=temp_h[:, :]))
        ev_sg = s.dma("sync", lambda h: h.dma_start(out=seg[:, :], in_=seg_h[:, :]))

        ev_t1 = s.do("vector", lambda h: h.tensor_scalar(srow2[0:1, 0:1], temp[0:1, 0:1],
                                                         CLAMP_MAX, None, ALU.min),
                     deps=[ev_tm])
        ev_t2 = s.do("scalar", lambda h: h.activation(srow2[0:1, 0:1], srow2[0:1, 0:1],
                                                      AF.Exp), deps=[ev_t1])
        ev_t3 = s.do("tensor", lambda h: h.matmul(g2_ps[:, 0:1], C["ones_row"][0:1, :],
                                                  srow2[0:1, 0:1], start=True, stop=True),
                     deps=[ev_t2, ev_c])
        ev_t4 = s.do("scalar", lambda h: h.copy(scal[:, :], g2_ps[:, 0:1]), deps=[ev_t3])

        ln_last = ln_prev["stats_mm"]
        ev_s1 = s.do("vector", lambda h: h.tensor_mul(gt[:, 0:4], sim[:, 0:4], sim[:, 0:4]),
                     deps=[ev_sim, ln_last])
        s.do("vector", lambda h: h.tensor_mul(gt[:, 4:8], sim[:, 4:8], sim[:, 4:8]))
        ev_s2 = s.do("vector", lambda h: h.tensor_copy(C["ln_accb"][:, 0:8], gt[:, 0:8]),
                     fence=True)
        s.do("tensor", lambda h: h.matmul(g2_ps[0:1, 8:12], C["ones_col"][:, 0:1],
                                          C["ln_accb"][:, 0:4], start=True, stop=True),
             deps=[ev_s2], mark=False)
        ev_s3 = s.do("tensor", lambda h: h.matmul(g2_ps[0:1, 8:12], C["ones_col"][:, 0:1],
                                                  C["ln_accb"][:, 4:8], start=False, stop=True))
        ev_s4 = s.do("vector", lambda h: h.tensor_scalar(srow2[0:1, 8:12], g2_ps[0:1, 8:12],
                                                         1e-24, None, ALU.max), deps=[ev_s3])
        ev_s5 = s.do("scalar", lambda h: h.activation(srow2[0:1, 8:12], srow2[0:1, 8:12],
                                                      AF.Ln), deps=[ev_s4])
        ev_s5 = s.do("scalar", lambda h: h.activation(srow2[0:1, 8:12], srow2[0:1, 8:12],
                                                      AF.Exp, scale=-0.5), fence=True)
        ev_s6 = s.do("tensor", lambda h: h.matmul(g2_ps[:, 8:12], C["ones_row"][0:1, :],
                                                  srow2[0:1, 8:12], start=True, stop=True),
                     deps=[ev_s5])
        ev_s7 = s.do("vector", lambda h: h.tensor_mul(simn[:, 0:4], sim[:, 0:4],
                                                      g2_ps[:, 8:12]), deps=[ev_s6], mark=False)
        ev_s7 = s.do("vector", lambda h: h.tensor_mul(simn[:, 4:8], sim[:, 4:8],
                                                      g2_ps[:, 8:12]))

        ev_ppre = {}
        prev_e = [None, None]
        for pc in range(2):
            for qb in range(NQB):
                p = (pc * NQB + qb) % 2
                t0 = qb * QB
                ev_mm = None
                for dc in range(NC):
                    ev_mm = s.do("tensor",
                                 (lambda pc, dc, t0, p: lambda h: h.matmul(
                                     mm_ps[p][:, :],
                                     wpT[:, dc * 256 + pc * 128: dc * 256 + pc * 128 + 128],
                                     xn[:, dc * T + t0: dc * T + t0 + QB],
                                     start=(dc == 0), stop=(dc == NC - 1)))(pc, dc, t0, p),
                                 deps=[ln_evs[qb], ev_wp, prev_e[p]] if dc == 0 else (),
                                 mark=(dc == NC - 1))
                ev_e = s.do("scalar",
                            (lambda pc, t0, p: lambda h: h.activation(
                                ppre[:, pc * T + t0: pc * T + t0 + QB], mm_ps[p][:, :],
                                AF.Identity, bias=bp[:, pc:pc + 1]))(pc, t0, p),
                            deps=[ev_mm, ev_bp])
                prev_e[p] = ev_e
                ev_ppre[(pc, qb)] = ev_e
        s.flush()

        ev_rs = {}
        prev_gmm = None
        prev_n2 = None
        prev_n4 = None
        for qb in range(NQB):
            t0 = qb * QB
            s.do("vector", (lambda t0: lambda h: h.tensor_mul(
                C["ln_sqv"][:, 0:512], ppre[:, t0:t0 + QB], ppre[:, t0:t0 + QB]))(t0),
                deps=[ev_ppre[(0, qb)], ev_ppre[(1, qb)], prev_gmm, ln_last], mark=False)
            s.do("vector", lambda h: h.tensor_copy(C["ln_accb"][:, 0:512],
                                                   C["ln_sqv"][:, 0:512]), mark=False)
            s.do("vector", (lambda t0: lambda h: h.tensor_mul(
                C["ln_sqv"][:, 0:512], ppre[:, T + t0: T + t0 + QB],
                ppre[:, T + t0: T + t0 + QB]))(t0), mark=False)
            ev_q2 = s.do("vector", lambda h: h.tensor_copy(C["ln_accb"][:, 512:1024],
                                                           C["ln_sqv"][:, 0:512]))
            s.do("tensor", lambda h: h.matmul(g_ps[0:1, 0:512], C["ones_col"][:, 0:1],
                                              C["ln_accb"][:, 0:512], start=True, stop=True),
                 deps=[ev_q2, prev_n2], mark=False)
            ev_n1 = s.do("tensor", lambda h: h.matmul(g_ps[0:1, 0:512], C["ones_col"][:, 0:1],
                                                      C["ln_accb"][:, 512:1024],
                                                      start=False, stop=True))
            ev_n2 = s.do("vector", lambda h: h.tensor_scalar(srow2[0:1, 0:512],
                                                             g_ps[0:1, 0:512], 1e-24, None,
                                                             ALU.max), deps=[ev_n1])
            prev_n2 = ev_n2
            ev_n3 = None
            for jj in range(4):
                ev_n3 = s.do("tensor",
                             (lambda jj: lambda h: h.matmul(
                                 g2_ps[:, 16 + jj:17 + jj], srow2[0:1, jj * 128:(jj + 1) * 128],
                                 C["ones_row"][0:1, 0:1], start=True, stop=True))(jj),
                             deps=[ev_n2, prev_n4] if jj == 0 else (), mark=jj == 3)
            prev_gmm = ev_n3
            ev_n4 = s.do("scalar", (lambda qb: lambda h: h.activation(
                rs[:, qb * 4:(qb + 1) * 4], g2_ps[:, 16:20], AF.Ln))(qb),
                deps=[ev_n3])
            ev_n4 = s.do("scalar", (lambda qb: lambda h: h.activation(
                rs[:, qb * 4:(qb + 1) * 4], rs[:, qb * 4:(qb + 1) * 4], AF.Exp,
                scale=-0.5))(qb), fence=True)
            prev_n4 = ev_n4
            ev_n5 = s.do("vector", (lambda qb: lambda h: h.tensor_scalar(
                rs[:, qb * 4:(qb + 1) * 4], rs[:, qb * 4:(qb + 1) * 4],
                scal[:, 0:1], None, ALU.mult))(qb), deps=[ev_n4, ev_t4])
            ev_rs[qb] = ev_n5
        s.flush()

        ev_retT = {}
        prev_lg = None
        prev_tr = None
        prev_l3 = None
        for tc in range(16):
            qb = tc // 4
            jj = tc % 4
            ev_mm = None
            for pc in range(2):
                ev_mm = s.do("tensor",
                             (lambda pc, tc: lambda h: h.matmul(
                                 g_ps[:, 512:516],
                                 ppre[:, pc * T + tc * 128: pc * T + tc * 128 + 128],
                                 simn[:, pc * 4:(pc + 1) * 4],
                                 start=(pc == 0), stop=(pc == 1)))(pc, tc),
                             deps=[ev_s7, prev_lg] if pc == 0 else (), mark=pc == 1)
            ev_l1 = s.do("vector", (lambda tc, qb, jj: lambda h: h.tensor_scalar(
                gt[:, 0:4], g_ps[:, 512:516], rs[:, qb * 4 + jj: qb * 4 + jj + 1],
                None, ALU.mult))(tc, qb, jj), deps=[ev_mm, ev_rs[qb], prev_l3])
            prev_lg = ev_l1
            s.do("vector", lambda h: h.reduce_max(gt[:, 4:5], gt[:, 0:4], axis=AX),
                 fence=True)
            ev_l2 = s.do("vector", lambda h: h.tensor_scalar(gt[:, 5:6], gt[:, 4:5],
                                                             -1.0, None, ALU.mult),
                         fence=True)
            ev_l3 = s.do("scalar", lambda h: h.activation(gt[:, 8:12], gt[:, 0:4], AF.Exp,
                                                          bias=gt[:, 5:6]), deps=[ev_l2])
            prev_l3 = ev_l3
            s.do("vector", lambda h: h.reduce_sum(gt[:, 6:7], gt[:, 8:12], axis=AX),
                 deps=[ev_l3], fence=True)
            s.do("vector", lambda h: h.reciprocal(gt[:, 7:8], gt[:, 6:7]), fence=True)
            s.do("vector", lambda h: h.tensor_scalar(gt[:, 12:16], gt[:, 0:4], gt[:, 4:5],
                                                     None, ALU.is_ge), fence=True)
            s.do("vector", lambda h: h.tensor_scalar(gt[:, 8:12], gt[:, 8:12], gt[:, 7:8],
                                                     None, ALU.mult), fence=True)
            ev_l4 = s.do("vector", lambda h: h.tensor_mul(retsb[:, 0:4], gt[:, 8:12],
                                                          gt[:, 12:16]),
                         deps=[prev_tr], fence=True)
            ev_l5 = s.do("tensor", (lambda tc: lambda h: h.matmul(
                g2_ps[0:4, 0:128], retsb[:, 0:4], ident[:, :], start=True, stop=True))(tc),
                deps=[ev_l4, ev_id, prev_tr])
            ev_l6 = s.do("scalar", (lambda tc: lambda h: h.copy(
                retT[0:4, tc * 128:(tc + 1) * 128], g2_ps[0:4, 0:128]))(tc), deps=[ev_l5])
            prev_tr = ev_l6
            ev_retT[tc] = ev_l6
        s.flush()

        prev_ts = None
        ev_x2 = []
        prev_mm = [None, None]
        for qb in range(NQB):
            t0 = qb * QB
            ev_sw = s.do("tensor", (lambda qb, t0: lambda h: h.matmul(
                g_ps[0:15, 512:1024], seg[0:4, :], retT[0:4, t0:t0 + QB],
                start=True, stop=True))(qb, t0),
                deps=[ev_sg, ev_retT[qb * 4 + 3], prev_ts])
            ev_tv = None
            for dc in range(NC):
                ev_tv = s.do("tensor",
                             (lambda dc, t0: lambda h: h.matmul(
                                 g_ps[0:15, 0:512], wdT[:, dc * 15:(dc + 1) * 15],
                                 xn[:, dc * T + t0: dc * T + t0 + QB],
                                 start=(dc == 0), stop=(dc == NC - 1)))(dc, t0),
                             deps=[ev_wd, prev_ts] if dc == 0 else (), mark=dc == NC - 1)
            ev_te = s.do("scalar", lambda h: h.copy(tvsb[0:15, :], g_ps[0:15, 0:512]),
                         deps=[ev_tv])
            ev_tsc = s.do("vector", lambda h: h.tensor_mul(tscsb[0:15, :], tvsb[0:15, :],
                                                           g_ps[0:15, 512:1024]),
                          deps=[ev_te, ev_sw])
            prev_ts = ev_tsc
            for oc in range(NC):
                p = (qb * NC + oc) % 2
                ev_mm = None
                for dc in range(NC):
                    ev_mm = s.do("tensor",
                                 (lambda oc, dc, t0, p: lambda h: h.matmul(
                                     mm_ps[p][:, :],
                                     wt[:, dc * D + oc * 128: dc * D + oc * 128 + 128],
                                     xn[:, dc * T + t0: dc * T + t0 + QB],
                                     start=(dc == 0), stop=False))(oc, dc, t0, p),
                                 deps=[ev_wt, prev_mm[p]] if dc == 0 else (), mark=False)
                ev_mm = s.do("tensor",
                             (lambda oc, t0, p: lambda h: h.matmul(
                                 mm_ps[p][:, :], wuT[0:15, oc * 128:(oc + 1) * 128],
                                 tscsb[0:15, :], start=False, stop=True))(oc, t0, p),
                             deps=[ev_tsc, ev_wu])
                ev_r = s.do("vector",
                            (lambda oc, t0, p: lambda h: h.tensor_add(
                                x[:, oc * T + t0: oc * T + t0 + QB], mm_ps[p][:, :],
                                x[:, oc * T + t0: oc * T + t0 + QB]))(oc, t0, p),
                            deps=[ev_mm])
                prev_mm[p] = ev_r
                ev_x2.append(ev_r)
        if dbg:
            s.dma("sync", lambda h: h.dma_start(out=dppre_h[:, :], in_=ppre[:, :]),
                  deps=[ev_ppre[(1, 3)]])
            s.dma("sync", lambda h: h.dma_start(out=drs_h[:, :], in_=rs[:, :]),
                  deps=[ev_rs[3]])
            s.dma("sync", lambda h: h.dma_start(out=dretT_h[:, :], in_=retT[0:4, :]),
                  deps=[ev_retT[15]])
            s.dma("sync", lambda h: h.dma_start(out=dtsc_h[:, :], in_=tscsb[0:15, :]),
                  deps=[prev_ts])
            s.dma("sync", lambda h: h.dma_start(out=dscal_h[:, :], in_=scal[:, :]),
                  deps=[ev_t4])
            s.dma("sync", lambda h: h.dma_start(out=dsimn_h[:, :], in_=simn[:, :]),
                  deps=[ev_s7])
            s.dma("sync", lambda h: h.dma_start(out=dxn_h[:, :], in_=xn[:, :]),
                  deps=[ln_evs[3]])
        s.flush()
        last_base_mm = ev_mm
        ln_prev2 = {}

    ev_x2_st = s.dma("sync", lambda h: h.dma_start(
        out=x2_h.ap().rearrange("(c p) t -> p c t", p=128), in_=x[:, :].rearrange("p (c x) -> p c x", c=8)),
        deps=[ev_x2[-1]])

    ln2, _ = emit_ln_all(s, C, x, xn, prev=ln_prev2, xn_war=last_base_mm)
    s.flush()

    with ExitStack() as ph:
        sbp = lambda n, sh, dt: ph.enter_context(nc.sbuf_tensor("sp_" + n, sh, dt))
        wk1 = sbp("wk1", [128, NC * D], BF16)
        stage = sbp("stage", [128, 4 * 512], BF16)
        qsq = sbp("qsq", [128, 1024], F32)
        esq = sbp("esq", [128, H], F32)
        esqb = sbp("esqb", [128, H], F32)
        psp = lambda n, sh: ph.enter_context(nc.psum_tensor("q_" + n, sh, F32))
        mm_ps = [psp("qm0", [128, 512]), psp("qm1", [128, 512])]
        tm_ps = [psp("tm0", [128, 1024]), psp("tm1", [128, 1024])]

        ev_wk1 = s.dma("gpsimd", lambda h: h.dma_start(
            out=wk1[:, :].rearrange("p (c x) -> p c x", c=8), in_=wkT_h.ap().rearrange("(c p) o -> p c o", p=128)))

        prev_evac = [None, None]
        prev_store = [None, None, None, None]
        slot = 0
        for oc in range(NC):
            for qb in range(NQB):
                p = slot % 2
                t0 = qb * QB
                ev_mm = None
                for dc in range(NC):
                    ev_mm = s.do("tensor",
                                 (lambda oc, dc, t0, p: lambda h: h.matmul(
                                     mm_ps[p][:, :],
                                     wk1[:, dc * D + oc * 128: dc * D + oc * 128 + 128],
                                     xn[:, dc * T + t0: dc * T + t0 + QB],
                                     start=(dc == 0), stop=(dc == NC - 1)))(oc, dc, t0, p),
                                 deps=[ln2[qb], ev_wk1, prev_evac[p]] if dc == 0 else (),
                                 mark=(dc == NC - 1))
                ev_e = s.do("scalar",
                            (lambda slot, p: lambda h: h.copy(
                                stage[:, (slot % 4) * 512:((slot % 4) + 1) * 512],
                                mm_ps[p][:, :]))(slot, p),
                            deps=[ev_mm, prev_store[slot % 4]])
                prev_evac[p] = ev_e
                ev_d = s.dma("sync",
                             (lambda slot, oc, t0: lambda h: h.dma_start(
                                 out=q1fm_h[oc * 128:(oc + 1) * 128, t0:t0 + QB],
                                 in_=stage[:, (slot % 4) * 512:((slot % 4) + 1) * 512]))(slot, oc, t0),
                             deps=[ev_e])
                prev_store[slot % 4] = ev_d
                slot += 1
        s.flush()
        marks = {"q1fm": (s.sem["dsync"], s.cnt["dsync"])}

        prev_done = [None, None]
        prev_stage = [None, None]
        prev_rd = None
        prev_ps = None
        prev_de = None
        for tc in range(16):
            p = tc % 2
            qb = (tc * 128) // QB
            ev_mm = None
            for half in range(2):
                for dc in range(NC):
                    ev_mm = s.do("tensor",
                                 (lambda tc, dc, half, p: lambda h: h.matmul(
                                     tm_ps[p][:, half * 512:(half + 1) * 512],
                                     xn[:, dc * T + tc * 128: dc * T + (tc + 1) * 128],
                                     wk1[:, dc * D + half * 512: dc * D + (half + 1) * 512],
                                     start=(dc == 0), stop=(dc == NC - 1)))(tc, dc, half, p),
                                 deps=[ln2[qb], prev_done[p]] if dc == 0 and half == 0 else (),
                                 mark=(dc == NC - 1 and half == 1))
            ev_sq = s.do("scalar", (lambda p: lambda h: h.square(qsq[:, 0:512],
                                                                 tm_ps[p][:, 0:512]))(p),
                         deps=[ev_mm, prev_rd], mark=False)
            ev_sq = s.do("scalar", (lambda p: lambda h: h.square(qsq[:, 512:1024],
                                                                 tm_ps[p][:, 512:1024]))(p))
            ev_rd = None
            for hh in range(H):
                ev_rd = s.do("vector",
                             (lambda hh: lambda h: h.reduce_sum(
                                 esq[:, hh:hh + 1], qsq[:, hh * 64:(hh + 1) * 64],
                                 axis=AX))(hh),
                             deps=[ev_sq, prev_de] if hh == 0 else (), mark=hh == H - 1)
            prev_rd = ev_rd
            ev_ex = s.do("vector", lambda h: h.tensor_scalar(
                esqb[:, :], esq[:, :], -0.125, -16.0, ALU.mult, ALU.add),
                deps=[ev_rd], fence=True)
            ev_de = s.dma("sync",
                          (lambda tc: lambda h: h.dma_start(
                              out=expsq_h[tc * 128:(tc + 1) * 128, :], in_=esqb[:, :]))(tc),
                          deps=[ev_ex])
            prev_de = ev_de
            ev_ps = s.do("scalar",
                         (lambda p: lambda h: h.copy(
                             stage[:, p * 1024: p * 1024 + D], tm_ps[p][:, 0:D]))(p),
                         deps=[ev_mm, prev_stage[p]] +
                         (list(prev_store) if tc < 2 else []))
            if isinstance(q1tm_h, list):
                ev_st = None
                for cch in range(4):
                    ev_st = s.dma("sync",
                                  (lambda tc, p, cch: lambda h: h.dma_start(
                                      out=q1tm_h[cch][tc * 128:(tc + 1) * 128, :],
                                      in_=stage[:, p * 1024 + cch * 256:
                                                p * 1024 + (cch + 1) * 256]))(tc, p, cch),
                                  deps=[ev_ps] if cch == 0 else ())
            else:
                ev_st = s.dma("sync",
                              (lambda tc, p: lambda h: h.dma_start(
                                  out=q1tm_h[tc * 128:(tc + 1) * 128, :],
                                  in_=stage[:, p * 1024: p * 1024 + D]))(tc, p),
                              deps=[ev_ps])
            prev_stage[p] = ev_st
            prev_done[p] = ev_st
    return marks


def p3_handles(nc, dbg=False, with_x=True):
    hh = {}
    if with_x:
        hh["x"] = nc.dram_tensor("x", [D, T], F32, kind="ExternalInput")
    hh["wT"] = nc.dram_tensor("wT", [D, D], F32, kind="ExternalInput")
    hh["wpT"] = nc.dram_tensor("wpT", [D, 256], F32, kind="ExternalInput")
    hh["bp"] = nc.dram_tensor("bp", [128, 2], F32, kind="ExternalInput")
    hh["sim"] = nc.dram_tensor("sim", [256, 4], F32, kind="ExternalInput")
    hh["temp"] = nc.dram_tensor("temp", [1, 1], F32, kind="ExternalInput")
    hh["wdT"] = nc.dram_tensor("wdT", [D, 15], F32, kind="ExternalInput")
    hh["wuT"] = nc.dram_tensor("wuT", [15, D], F32, kind="ExternalInput")
    hh["wk1T"] = nc.dram_tensor("wk1T", [D, D], F32, kind="ExternalInput")
    hh["ones_col"] = nc.dram_tensor("ones_col", [128, 1], BF16, kind="ExternalInput")
    hh["ones_row"] = nc.dram_tensor("ones_row", [1, 128], F32, kind="ExternalInput")
    hh["segmat4"] = nc.dram_tensor("segmat4", [4, 15], BF16, kind="ExternalInput")
    hh["ident_bf"] = nc.dram_tensor("ident_bf", [128, 128], BF16, kind="ExternalInput")
    hh["x2"] = nc.dram_tensor("x2", [D, T], F32, kind="ExternalOutput")
    hh["q1fm"] = nc.dram_tensor("q1fm", [D, T], BF16, kind="ExternalOutput")
    hh["q1tm"] = nc.dram_tensor("q1tm", [T, D], BF16, kind="ExternalOutput")
    hh["nb"] = nc.dram_tensor("nb", [T, H], F32, kind="ExternalOutput")
    if dbg:
        hh["d_ppre"] = nc.dram_tensor("d_ppre", [128, 2 * T], BF16, kind="ExternalOutput")
        hh["d_rs"] = nc.dram_tensor("d_rs", [128, 16], F32, kind="ExternalOutput")
        hh["d_retT"] = nc.dram_tensor("d_retT", [4, T], BF16, kind="ExternalOutput")
        hh["d_tsc"] = nc.dram_tensor("d_tsc", [15, 512], BF16, kind="ExternalOutput")
        hh["d_scal"] = nc.dram_tensor("d_scal", [128, 1], F32, kind="ExternalOutput")
        hh["d_simn"] = nc.dram_tensor("d_simn", [128, 8], BF16, kind="ExternalOutput")
        hh["d_xn"] = nc.dram_tensor("d_xn", [128, NC * T], BF16, kind="ExternalOutput")
    return hh


def build_p3(dbg=False):
    nc = bass.Bass("TRN2", target_bir_lowering=False)
    hh = p3_handles(nc, dbg=dbg)
    with ExitStack() as stack:
        s = Sch(nc, stack)
        emit_p3_body(nc, stack, s, hh, x=None, dbg=dbg)
        s.final_wait()
    return nc



def build_p4(fuse=False):
    nc = bass.Bass("TRN2", target_bir_lowering=False)
    x_h = nc.dram_tensor("x", [D, T], F32, kind="ExternalInput")
    qown_h = nc.dram_tensor("qown", [D, T], BF16, kind="ExternalInput")
    kfull_h = nc.dram_tensor("kfull", [D, S], BF16, kind="ExternalInput")
    vall_h = nc.dram_tensor("vall", [HP, 128, S], BF16, kind="ExternalInput")
    esq_h = nc.dram_tensor("nball", [HP, 128, 128], F32, kind="ExternalInput")
    wkT_h = nc.dram_tensor("wkT", [D, D], F32, kind="ExternalInput")
    wvT_h = nc.dram_tensor("wvT", [D, D], F32, kind="ExternalInput")
    woT_h = nc.dram_tensor("woT", [D, D], F32, kind="ExternalInput")
    onesall_h = nc.dram_tensor("ones_all", [128, 128], F32, kind="ExternalInput")
    aug1_h = nc.dram_tensor("aug1", [128, 1], BF16, kind="ExternalInput")
    if fuse:
        hh5 = p5_handles(nc, with_x=False)
        x3_h = None
    else:
        x3_h = nc.dram_tensor("x3", [D, T], F32, kind="ExternalOutput")

    with ExitStack() as stack:
        sb = lambda n, sh, dt: stack.enter_context(nc.sbuf_tensor("s_" + n, sh, dt))
        psf = lambda n, sh: stack.enter_context(nc.psum_tensor("q_" + n, sh, F32))
        s = Sch(nc, stack)

        x_outer = sb("x", [128, NC * T], F32) if fuse else None
        p4s = ExitStack()
        sb4 = lambda n, sh, dt: p4s.enter_context(nc.sbuf_tensor("s4_" + n, sh, dt))
        zcat = sb4("zcat", [128, HP * T], BF16)
        avcat = zcat
        merged = sb4("mg", [128, HP * 64], BF16)
        C = {}
        C["ones_all"] = sb4("onesall", [128, 128], F32)
        aug1 = sb4("aug1", [128, 1], BF16)
        C["recipb"] = sb4("recipb", [128, 1024], F32)
        C["rb_sb"] = sb4("rb", [128, 1024], BF16)

        s.dma("sync", lambda h: h.dma_start(out=C["ones_all"][:, :], in_=onesall_h[:, :]))
        s.dma("sync", lambda h: h.dma_start(out=aug1[:, :], in_=aug1_h[:, :]))

        with ExitStack() as ph:
            sbp = lambda n, sh, dt: ph.enter_context(nc.sbuf_tensor("sp_" + n, sh, dt))
            wkv = sbp("wkv", [128, NC * D], BF16)
            wvv = sbp("wvv", [128, NC * D], BF16)
            psp = lambda n, sh: ph.enter_context(nc.psum_tensor("q_" + n, sh, F32))
            mg_ps = psp("mgps", [128, 512])
            ev_wk = s.dma("gpsimd", lambda h: h.dma_start(
                out=wkv[:, :].rearrange("p (c x) -> p c x", c=8), in_=wkT_h.ap().rearrange("(c p) o -> p c o", p=128)))
            ev_wv = s.dma("gpsimd", lambda h: h.dma_start(
                out=wvv[:, :].rearrange("p (c x) -> p c x", c=8), in_=wvT_h.ap().rearrange("(c p) o -> p c o", p=128)))
            ev_mg = None
            for hp in range(HP):
                for hh in range(2):
                    hd = hp * 128 + hh * 64
                    for dc in range(NC):
                        ev_mg = s.do("tensor",
                                     (lambda hp, hh, hd, dc: lambda h: h.matmul(
                                         mg_ps[hh * 64:(hh + 1) * 64, hp * 64:(hp + 1) * 64],
                                         wkv[:, dc * D + hd: dc * D + hd + 64],
                                         wvv[:, dc * D + hd: dc * D + hd + 64],
                                         start=(dc == 0), stop=(dc == NC - 1),
                                         tile_position=(0, 64 * hh)))(hp, hh, hd, dc),
                                     deps=[ev_wk, ev_wv] if (hp == 0 and hh == 0 and dc == 0) else (),
                                     mark=(dc == NC - 1))
            ev_mge = s.do("scalar", lambda h: h.copy(merged[:, :], mg_ps[:, 0:512]),
                          deps=[ev_mg])
            s.flush()

        with ExitStack() as ph:
            sbp = lambda n, sh, dt: ph.enter_context(nc.sbuf_tensor("sp_" + n, sh, dt))
            qown = sbp("qown", [128, HP * T], BF16)
            kside = sbp("kside", [128, S], BF16)
            vbuf = sbp("vbuf", [128, S], BF16)
            esqs = sbp("esqs", [128, 128], F32)
            psp = lambda n, sh: ph.enter_context(nc.psum_tensor("q_" + n, sh, F32))
            C["pbuf"] = sbp("pbuf", [128, 2048], BF16)
            C["sc0"] = psp("sc0", [128, 1024])
            C["sc1"] = psp("sc1", [128, 1024])
            C["pv"] = psp("pv", [128, 1024])
            C["smp"] = psp("smp", [128, 1024])

            ev_q = s.dma("sync", lambda h: h.dma_start(
                out=qown[:, :].rearrange("p (c x) -> p c x", c=8), in_=qown_h.ap().rearrange("(c p) t -> p c t", p=128)))

            carry = {"qown_ev": ev_q}
            last = None
            for hp in range(HP):
                deps_k = [] if hp == 0 else [last["score"]]
                deps_v = [] if hp == 0 else [last["pv"]]
                deps_e = [] if hp == 0 else [last["sum"]]
                ev_k = s.dma("sync",
                             (lambda hp: lambda h: h.dma_start(
                                 out=kside[:, :], in_=kfull_h[hp * 128:(hp + 1) * 128, :]))(hp),
                             deps=deps_k)
                ev_v = s.dma("sync",
                             (lambda hp: lambda h: h.dma_start(
                                 out=vbuf[:, :], in_=vall_h[hp, :, :]))(hp),
                             deps=deps_v)
                ev_e = s.dma("sync",
                             (lambda hp: lambda h: h.dma_start(
                                 out=esqs[:, :], in_=esq_h[hp, :, :]))(hp),
                             deps=deps_e)
                last = emit_attention_hp(
                    s, C, hp, [(qb * QB, [(g, None) for g in range(64)])
                               for qb in range(NQB)],
                    0.25, kside, qown, vbuf,
                    aug=lambda kt, hh: aug1[:, 0:1],
                    masks_h=None, zcat=zcat, ev_kv=(ev_k, ev_v, ev_e), carry=carry,
                    biasf=lambda kt, hh: esqs[:, kt * 2 + hh: kt * 2 + hh + 1])
            ev_attn_done = last["znlast"]
            s.flush()

        with ExitStack() as ph:
            sbp = lambda n, sh, dt: ph.enter_context(nc.sbuf_tensor("sp_" + n, sh, dt))
            psp = lambda n, sh: ph.enter_context(nc.psum_tensor("q_" + n, sh, F32))
            av_ps = [psp("av0", [128, 512]), psp("av1", [128, 512])]
            prev_av = [None, None]
            ev_av_all = None
            for hp in range(HP):
                for qb in range(NQB):
                    t0 = qb * QB
                    p = (hp * NQB + qb) % 2
                    deps0 = [ev_mge, prev_av[p]]
                    if hp == 0 and qb == 0:
                        deps0.append(ev_attn_done)
                    s.do("tensor",
                         (lambda hp, t0, p: lambda h: h.matmul(
                             av_ps[p][0:64, :], merged[0:64, hp * 64:(hp + 1) * 64],
                             zcat[0:64, hp * T + t0: hp * T + t0 + QB],
                             start=True, stop=True, tile_position=(0, 0)))(hp, t0, p),
                         deps=deps0, mark=False)
                    ev_m2 = s.do("tensor",
                                 (lambda hp, t0, p: lambda h: h.matmul(
                                     av_ps[p][64:128, :], merged[64:128, hp * 64:(hp + 1) * 64],
                                     zcat[64:128, hp * T + t0: hp * T + t0 + QB],
                                     start=True, stop=True, tile_position=(64, 64)))(hp, t0, p))
                    ev_ae = s.do("scalar",
                                 (lambda hp, t0, p: lambda h: h.activation(
                                     avcat[:, hp * T + t0: hp * T + t0 + QB], av_ps[p][:, :],
                                     AF.Copy, scale=0.125))(hp, t0, p),
                                 deps=[ev_m2])
                    prev_av[p] = ev_ae
                    ev_av_all = ev_ae
            s.flush()

        with ExitStack() as ph:
            sbp = lambda n, sh, dt: ph.enter_context(nc.sbuf_tensor("sp_" + n, sh, dt))
            x = x_outer if fuse else sbp("x", [128, NC * T], F32)
            pw = sbp("pw", [128, NC * D], BF16)
            xstage = sbp("xstage", [128, 2 * 512], F32)
            psp = lambda n, sh: ph.enter_context(nc.psum_tensor("q_" + n, sh, F32))
            mm_ps = [psp("pm0", [128, 512]), psp("pm1", [128, 512])]
            ev_x = s.dma("sync", lambda h: h.dma_start(
                out=x[:, :].rearrange("p (c x) -> p c x", c=8), in_=x_h.ap().rearrange("(c p) t -> p c t", p=128)))
            ev_pw = s.dma("gpsimd", lambda h: h.dma_start(
                out=pw[:, :].rearrange("p (c x) -> p c x", c=8), in_=woT_h.ap().rearrange("(c p) o -> p c o", p=128)))
            emit_proj_residual(s, C, pw, avcat, x, x3_h, mm_ps, xstage,
                               ev_av_all, ev_pw, ev_x)
            s.flush()
        if fuse:
            s.barrier()
            s.flush()
            p4s.close()
            emit_p5_body(nc, stack, s, hh5, x=x_outer)
        else:
            p4s.close()
        s.final_wait()
    return nc



def emit_p5_body(nc, stack, s, hh, x=None):
    (x_h, wr1_h, br1_h, wr2_h, br2_h, ws1_h, bs1_h, ws2_h, bs2_h, we1_h,
     be1_h, we2_h, be2_h, onesc_h, onesr_h, onesrb_h, ident_h, out_h) = (
        hh.get("x"), hh["wr1"], hh["br1"], hh["wr2"], hh["br2_row"], hh["ws1"],
        hh["bs1"], hh["ws2"], hh["bs2"], hh["we1"], hh["be1"], hh["we2"],
        hh["be2_row"], hh["ones_col5"], hh["ones_row5"], hh["ones_row_bf"],
        hh["ident_bf5"], hh["out"])
    sb = lambda n, sh, dt: stack.enter_context(nc.sbuf_tensor("s5_" + n, sh, dt))
    psf = lambda n, sh: stack.enter_context(nc.psum_tensor("q5_" + n, sh, F32))
    xn = sb("xn", [128, NC * T], BF16)
    if x is None:
        x = sb("x", [128, NC * T], F32)
        ev_x = s.dma("sync", lambda h: h.dma_start(
            out=x[:, :].rearrange("p (c x) -> p c x", c=8),
            in_=x_h.ap().rearrange("(c p) t -> p c t", p=128)))
    else:
        ev_x = None
    onesrb = sb("onesrb", [1, 128], BF16)
    ident = sb("ident", [128, 128], BF16)
    wT = sb("wT", [8, T], BF16)
    wrow = sb("wrow", [1, 1024], BF16)
    wb = sb("wb", [128, 1024], BF16)
    be2r = sb("be2r", [1, 8 * D], BF16)
    bs1 = sb("bs1", [128, 32], F32)
    bs2 = sb("bs2", [128, NC], F32)
    be1 = sb("be1", [128, 8 * 32], F32)

    s.dma("sync", lambda h: h.dma_start(out=onesrb[:, :], in_=onesrb_h[:, :]))
    ev_id = s.dma("sync", lambda h: h.dma_start(out=ident[:, :], in_=ident_h[:, :]))
    s.dma("sync", lambda h: h.dma_start(out=bs1[:, :], in_=bs1_h[:, :]))
    s.dma("sync", lambda h: h.dma_start(out=bs2[:, :], in_=bs2_h[:, :]))
    ev_be2 = s.dma("gpsimd", lambda h: h.dma_start(out=be2r[:, :], in_=be2_h[:, :]))
    ev_be1 = s.dma("gpsimd", lambda h: h.dma_start(
        out=be1[:, :].rearrange("p (e c) -> p e c", e=8), in_=be1_h.ap().rearrange("e p c -> p e c")))

    with ExitStack() as lnph:
        sbl = lambda n, sh, dt: lnph.enter_context(nc.sbuf_tensor("sl_" + n, sh, dt))
        psl = lambda n, sh: lnph.enter_context(nc.psum_tensor("ql_" + n, sh, F32))
        C = ln_scratch(lnph, nc, sbl, psl)
        C["ones_col"] = sbl("onesc", [128, 1], BF16)
        C["ones_row"] = sbl("onesr", [1, 128], F32)
        s.dma("sync", lambda h: h.dma_start(out=C["ones_col"][:, :], in_=onesc_h[:, :]))
        s.dma("sync", lambda h: h.dma_start(out=C["ones_row"][:, :], in_=onesr_h[:, :]))
        ln_evs, ln_prev = emit_ln_all(s, C, x, xn, ev_x)
        s.flush()

    with ExitStack() as ph:
        sbp = lambda n, sh, dt: ph.enter_context(nc.sbuf_tensor("sp5_" + n, sh, dt))
        wr1 = sbp("wr1s", [128, NC * 256], BF16)
        wr2 = sbp("wr2s", [128, 2 * 8], BF16)
        br1 = sbp("br1s", [128, 2], F32)
        br2b = sbp("br2b", [128, 8], F32)
        br2r = sbp("br2r", [1, 8], F32)
        r1 = sbp("r1", [128, 2 * T], BF16)
        gt = sbp("gt", [128, 40], F32)
        wfull = sbp("wfull", [128, 8], BF16)
        psp = lambda n, sh: ph.enter_context(nc.psum_tensor("q5_" + n, sh, F32))
        g_ps = psp("gps", [128, 512])
        mm_ps = [psp("rm0", [128, 512]), psp("rm1", [128, 512])]

        ev_w1 = s.dma("gpsimd", lambda h: h.dma_start(
            out=wr1[:, :].rearrange("p (c x) -> p c x", c=8), in_=wr1_h.ap().rearrange("(c p) o -> p c o", p=128)))
        ev_w2 = s.dma("gpsimd", lambda h: h.dma_start(
            out=wr2[:, :].rearrange("p (c x) -> p c x", c=2), in_=wr2_h.ap().rearrange("(c p) o -> p c o", p=128)))
        ev_b1 = s.dma("sync", lambda h: h.dma_start(out=br1[:, :], in_=br1_h[:, :]))
        ev_b2 = s.dma("sync", lambda h: h.dma_start(out=br2r[:, :], in_=br2_h[:, :]))

        ev_bb = s.do("tensor", lambda h: h.matmul(g_ps[:, 0:8], C["ones_row"][0:1, :],
                                                  br2r[0:1, :], start=True, stop=True),
                     deps=[ev_b2])
        ev_bb = s.do("scalar", lambda h: h.copy(br2b[:, :], g_ps[:, 0:8]), deps=[ev_bb])

        prev_e = [None, None]
        ev_r1 = {}
        for pc in range(2):
            for qb in range(NQB):
                p = (pc * NQB + qb) % 2
                t0 = qb * QB
                ev_mm = None
                for dc in range(NC):
                    ev_mm = s.do("tensor",
                                 (lambda pc, dc, t0, p: lambda h: h.matmul(
                                     mm_ps[p][:, :],
                                     wr1[:, dc * 256 + pc * 128: dc * 256 + pc * 128 + 128],
                                     xn[:, dc * T + t0: dc * T + t0 + QB],
                                     start=(dc == 0), stop=(dc == NC - 1)))(pc, dc, t0, p),
                                 deps=[ln_evs[qb], ev_w1, prev_e[p]] if dc == 0 else (),
                                 mark=(dc == NC - 1))
                ev_e = s.do("scalar",
                            (lambda pc, t0, p: lambda h: h.activation(
                                r1[:, pc * T + t0: pc * T + t0 + QB], mm_ps[p][:, :],
                                AF.Relu, bias=br1[:, pc:pc + 1]))(pc, t0, p),
                            deps=[ev_mm, ev_b1])
                prev_e[p] = ev_e
                ev_r1[(pc, qb)] = ev_e
        s.flush()

        prev_lg = None
        prev_tr = None
        prev_sc = None
        ev_wT = None
        for tc in range(16):
            qb = tc // 4
            ev_mm = None
            for pc in range(2):
                ev_mm = s.do("tensor",
                             (lambda pc, tc: lambda h: h.matmul(
                                 g_ps[:, 16:24],
                                 r1[:, pc * T + tc * 128: pc * T + tc * 128 + 128],
                                 wr2[:, pc * 8:(pc + 1) * 8],
                                 start=(pc == 0), stop=(pc == 1)))(pc, tc),
                             deps=[ev_r1[(0, qb)], ev_r1[(1, qb)], ev_w2, prev_lg]
                             if pc == 0 else (), mark=pc == 1)
            ev_l0 = s.do("vector", lambda h: h.tensor_add(gt[:, 0:8], g_ps[:, 16:24],
                                                          br2b[:, :]),
                         deps=[ev_mm, ev_bb, prev_sc])
            prev_lg = ev_l0
            s.do("vector", lambda h: h.reduce_max(gt[:, 8:9], gt[:, 0:8], axis=AX),
                 fence=True)
            ev_l1 = s.do("vector", lambda h: h.tensor_scalar(gt[:, 9:10], gt[:, 8:9], -1.0,
                                                             None, ALU.mult), fence=True)
            ev_l2 = s.do("scalar", lambda h: h.activation(gt[:, 16:24], gt[:, 0:8], AF.Exp,
                                                          bias=gt[:, 9:10]), deps=[ev_l1])
            s.do("vector", lambda h: h.reduce_sum(gt[:, 10:11], gt[:, 16:24], axis=AX),
                 deps=[ev_l2], fence=True)
            s.do("vector", lambda h: h.reciprocal(gt[:, 11:12], gt[:, 10:11]), fence=True)
            s.do("vector", lambda h: h.tensor_scalar(gt[:, 16:24], gt[:, 16:24],
                                                     gt[:, 11:12], None, ALU.mult),
                 fence=True)
            s.do("vector", lambda h: h.reduce_max(gt[:, 8:9], gt[:, 16:24], axis=AX),
                 fence=True)
            s.do("vector", lambda h: h.tensor_scalar(gt[:, 24:32], gt[:, 16:24], gt[:, 8:9],
                                                     None, ALU.is_ge), fence=True)
            s.do("vector", lambda h: h.scalar_tensor_tensor(gt[:, 32:40], gt[:, 24:32],
                                                            -1e9, gt[:, 16:24], ALU.mult,
                                                            ALU.add), fence=True)
            s.do("vector", lambda h: h.reduce_max(gt[:, 12:13], gt[:, 32:40], axis=AX),
                 fence=True)
            ev_l3 = s.do("vector", lambda h: h.tensor_scalar(gt[:, 32:40], gt[:, 32:40],
                                                             gt[:, 12:13], None, ALU.is_ge),
                         fence=True)
            ev_l4 = s.do("scalar", lambda h: h.activation(gt[:, 13:14], gt[:, 8:9], AF.Exp,
                                                          scale=0.5), deps=[ev_l3])
            ev_l4 = s.do("scalar", lambda h: h.activation(gt[:, 14:15], gt[:, 12:13], AF.Exp,
                                                          scale=0.5))
            prev_sc = ev_l4
            s.do("vector", lambda h: h.tensor_add(gt[:, 15:16], gt[:, 13:14], gt[:, 14:15]),
                 deps=[ev_l4], fence=True)
            s.do("vector", lambda h: h.reciprocal(gt[:, 15:16], gt[:, 15:16]), fence=True)
            s.do("vector", lambda h: h.tensor_mul(gt[:, 13:14], gt[:, 13:14], gt[:, 15:16]),
                 fence=True)
            s.do("vector", lambda h: h.tensor_mul(gt[:, 14:15], gt[:, 14:15], gt[:, 15:16]),
                 fence=True)
            s.do("vector", lambda h: h.tensor_scalar(gt[:, 24:32], gt[:, 24:32],
                                                     gt[:, 13:14], None, ALU.mult),
                 fence=True)
            ev_l5 = s.do("vector", lambda h: h.scalar_tensor_tensor(
                wfull[:, 0:8], gt[:, 32:40], gt[:, 14:15], gt[:, 24:32], ALU.mult, ALU.add),
                deps=[prev_tr], fence=True)
            ev_l6 = s.do("tensor", (lambda tc: lambda h: h.matmul(
                g_ps[0:8, 128:256], wfull[:, 0:8], ident[:, :], start=True, stop=True))(tc),
                deps=[ev_l5, ev_id])
            ev_l7 = s.do("scalar", (lambda tc: lambda h: h.copy(
                wT[0:8, tc * 128:(tc + 1) * 128], g_ps[0:8, 128:256]))(tc), deps=[ev_l6])
            prev_tr = ev_l7
            ev_wT = ev_l7
        s.flush()

    with ExitStack() as ph:
        sbp = lambda n, sh, dt: ph.enter_context(nc.sbuf_tensor("sp5_" + n, sh, dt))
        w1c = sbp("w1c", [128, 3 * NC * 128], BF16)
        w2c = sbp("w2c", [128, 2 * 32 * 128], BF16)
        silu = sbp("silu", [128, 32 * 1024], BF16)
        psp = lambda n, sh: ph.enter_context(nc.psum_tensor("q5_" + n, sh, F32))
        a_ps = [psp("a0", [128, 1024]), psp("a1", [128, 1024])]
        h_ps = [psp("h0", [128, 1024]), psp("h1", [128, 1024])]

        ev_out_last = None
        prev_a_evac = [None, None]
        prev_h = [None, None]
        prev_w1 = {0: None, 1: None, 2: None}
        prev_rank1 = None
        prev_wb_read = None
        for half in range(2):
            th0 = half * 1024
            for e in range(9):
                shared = e == 0
                if not shared:
                    ev_wr = s.dma("gpsimd",
                                  (lambda e, th0: lambda h: h.dma_start(
                                      out=wrow[0:1, :],
                                      in_=wT[e - 1:e, th0:th0 + 1024]))(e, th0),
                                  deps=[ev_wT, prev_rank1])
                    s.do("tensor",
                         (lambda th0: lambda h: h.matmul(
                             a_ps[0][:, 0:512], onesrb[0:1, :],
                             wrow[0:1, 0:512], start=True, stop=True))(th0),
                         deps=[ev_wr, prev_a_evac[0]], mark=False)
                    ev_bc2 = s.do("tensor",
                                  (lambda th0: lambda h: h.matmul(
                                      a_ps[0][:, 512:1024], onesrb[0:1, :],
                                      wrow[0:1, 512:1024],
                                      start=True, stop=True))(th0))
                    ev_wb = s.do("scalar", lambda h: h.copy(wb[:, :], a_ps[0][:, :]),
                                 deps=[ev_bc2, prev_wb_read])
                else:
                    ev_wb = None

                ev_silu = None
                for ic in range(32):
                    wslot = ic % 3
                    if shared:
                        src = ws1_h.ap()[:, ic * 128:(ic + 1) * 128].rearrange(
                            "(c p) i -> p c i", p=128)
                    else:
                        src = we1_h.ap()[e - 1, :, ic * 128:(ic + 1) * 128].rearrange(
                            "(c p) i -> p c i", p=128)
                    ev_w1 = s.dma("gpsimd",
                                  (lambda src, wslot: lambda h: h.dma_start(
                                      out=w1c[:, wslot * NC * 128:(wslot + 1) * NC * 128]
                                      .rearrange("p (c i) -> p c i", c=NC),
                                      in_=src))(src, wslot),
                                  deps=[prev_w1[wslot]])
                    p = ic % 2
                    ev_mm = None
                    for tb in range(2):
                        for dc in range(NC):
                            ev_mm = s.do("tensor",
                                         (lambda ic, tb, dc, p, th0, wslot: lambda h: h.matmul(
                                             a_ps[p][:, tb * 512:(tb + 1) * 512],
                                             w1c[:, wslot * NC * 128 + dc * 128:
                                                 wslot * NC * 128 + dc * 128 + 128],
                                             xn[:, dc * T + th0 + tb * 512:
                                                dc * T + th0 + (tb + 1) * 512],
                                             start=(dc == 0), stop=(dc == NC - 1)))(
                                             ic, tb, dc, p, th0, wslot),
                                         deps=[ev_w1, prev_a_evac[p],
                                               (ev_wb if ic < 2 else None)]
                                         if dc == 0 else (),
                                         mark=(dc == NC - 1 and tb == 1))
                    prev_w1[wslot] = ev_mm
                    bias = bs1[:, ic:ic + 1] if shared else be1[:, (e - 1) * 32 + ic:(e - 1) * 32 + ic + 1]
                    ev_sl = s.do("scalar",
                                 (lambda ic, p, bias: lambda h: h.activation(
                                     silu[:, ic * 1024:(ic + 1) * 1024], a_ps[p][:, :],
                                     AF.Silu, bias=bias))(ic, p, bias),
                                 deps=[ev_mm, ev_be1] +
                                 ([prev_h[0], prev_h[1]] if ic == 0 else []))
                    if not shared:
                        ev_sl = s.do("vector",
                                     (lambda ic: lambda h: h.tensor_mul(
                                         silu[:, ic * 1024:(ic + 1) * 1024],
                                         silu[:, ic * 1024:(ic + 1) * 1024], wb[:, :]))(ic),
                                     deps=[ev_sl])
                        prev_wb_read = ev_sl
                    prev_a_evac[p] = ev_sl
                    ev_silu = ev_sl
                for oc in range(NC):
                    p = oc % 2
                    if shared:
                        src2 = ws2_h.ap()[:, oc * 128:(oc + 1) * 128].rearrange(
                            "(c p) o -> p c o", p=128)
                    else:
                        src2 = we2_h.ap()[e - 1, :, oc * 128:(oc + 1) * 128].rearrange(
                            "(c p) o -> p c o", p=128)
                    ev_w2 = s.dma("gpsimd",
                                  (lambda src2, p: lambda h: h.dma_start(
                                      out=w2c[:, p * 32 * 128:(p + 1) * 32 * 128]
                                      .rearrange("p (c o) -> p c o", c=32),
                                      in_=src2))(src2, p),
                                  deps=[prev_h[p]])
                    ev_hm = [None, None]
                    for ic in range(32):
                        last_shared = shared and ic == 31
                        for tb in range(2):
                            ev_hm[tb] = s.do("tensor",
                                             (lambda oc, ic, tb, p: lambda h: h.matmul(
                                                 h_ps[p][:, tb * 512:(tb + 1) * 512],
                                                 w2c[:, p * 32 * 128 + ic * 128:
                                                     p * 32 * 128 + ic * 128 + 128],
                                                 silu[:, ic * 1024 + tb * 512:
                                                      ic * 1024 + (tb + 1) * 512],
                                                 start=(ic == 0), stop=last_shared))(oc, ic, tb, p),
                                             deps=[ev_w2, ev_silu, prev_h[p]]
                                             if ic == 0 and tb == 0 else (),
                                             mark=last_shared)
                    if not shared:
                        for tb in range(2):
                            ev_hm[tb] = s.do("tensor",
                                             (lambda e, oc, tb, p: lambda h: h.matmul(
                                                 h_ps[p][:, tb * 512:(tb + 1) * 512],
                                                 be2r[0:1, (e - 1) * D + oc * 128:
                                                      (e - 1) * D + oc * 128 + 128],
                                                 wrow[0:1, tb * 512:(tb + 1) * 512],
                                                 start=False, stop=True))(e, oc, tb, p),
                                             deps=[ev_be2] if tb == 0 else (), mark=True)
                        prev_rank1 = ev_hm[1]
                    for tb in range(2):
                        t0a = th0 + tb * 512
                        if shared:
                            ev_ev = s.do("vector",
                                         (lambda oc, tb, p, t0a: lambda h: h.scalar_tensor_tensor(
                                             x[:, oc * T + t0a: oc * T + t0a + 512],
                                             h_ps[p][:, tb * 512:(tb + 1) * 512],
                                             bs2[:, oc:oc + 1],
                                             x[:, oc * T + t0a: oc * T + t0a + 512],
                                             ALU.add, ALU.add))(oc, tb, p, t0a),
                                         deps=[ev_hm[tb]], mark=tb == 1)
                        else:
                            ev_ev = s.do("vector",
                                         (lambda oc, tb, p, t0a: lambda h: h.tensor_add(
                                             x[:, oc * T + t0a: oc * T + t0a + 512],
                                             h_ps[p][:, tb * 512:(tb + 1) * 512],
                                             x[:, oc * T + t0a: oc * T + t0a + 512]))(
                                             oc, tb, p, t0a),
                                         deps=[ev_hm[tb]], mark=tb == 1)
                    prev_h[p] = ev_ev
                    ev_out_last = ev_ev
        s.dma("sync", lambda h: h.dma_start(
            out=out_h.ap().rearrange("(c p) t -> p c t", p=128), in_=x[:, :].rearrange("p (c x) -> p c x", c=8)),
            deps=[ev_out_last])
        s.flush()


def p5_handles(nc, with_x=True, shared_consts=None):
    hh = {}
    if with_x:
        hh["x"] = nc.dram_tensor("x", [D, T], F32, kind="ExternalInput")
    hh["wr1"] = nc.dram_tensor("wr1", [D, 256], F32, kind="ExternalInput")
    hh["br1"] = nc.dram_tensor("br1", [128, 2], F32, kind="ExternalInput")
    hh["wr2"] = nc.dram_tensor("wr2", [256, 8], F32, kind="ExternalInput")
    hh["br2_row"] = nc.dram_tensor("br2_row", [1, 8], F32, kind="ExternalInput")
    hh["ws1"] = nc.dram_tensor("ws1", [D, 4096], F32, kind="ExternalInput")
    hh["bs1"] = nc.dram_tensor("bs1", [128, 32], F32, kind="ExternalInput")
    hh["ws2"] = nc.dram_tensor("ws2", [4096, D], F32, kind="ExternalInput")
    hh["bs2"] = nc.dram_tensor("bs2", [128, NC], F32, kind="ExternalInput")
    hh["we1"] = nc.dram_tensor("we1", [8, D, 4096], F32, kind="ExternalInput")
    hh["be1"] = nc.dram_tensor("be1", [8, 128, 32], F32, kind="ExternalInput")
    hh["we2"] = nc.dram_tensor("we2", [8, 4096, D], F32, kind="ExternalInput")
    hh["be2_row"] = nc.dram_tensor("be2_row", [1, 8 * D], F32, kind="ExternalInput")
    if shared_consts is None:
        hh["ones_col5"] = nc.dram_tensor("ones_col", [128, 1], BF16, kind="ExternalInput")
        hh["ones_row5"] = nc.dram_tensor("ones_row", [1, 128], F32, kind="ExternalInput")
        hh["ident_bf5"] = nc.dram_tensor("ident_bf", [128, 128], BF16, kind="ExternalInput")
    else:
        hh["ones_col5"] = shared_consts["ones_col"]
        hh["ones_row5"] = shared_consts["ones_row"]
        hh["ident_bf5"] = shared_consts["ident_bf"]
    hh["ones_row_bf"] = nc.dram_tensor("ones_row_bf", [1, 128], BF16, kind="ExternalInput")
    hh["out"] = nc.dram_tensor("out", [D, T], F32, kind="ExternalOutput")
    return hh


def build_p5():
    nc = bass.Bass("TRN2", target_bir_lowering=False)
    hh = p5_handles(nc)
    with ExitStack() as stack:
        s = Sch(nc, stack)
        emit_p5_body(nc, stack, s, hh, x=None)
        s.final_wait()
    return nc



RG4 = [[0, 1, 2, 3], [4, 5, 6, 7]]


def emit_p1_body(nc, stack, s, hh, x):
    x_h = hh["x"]
    wqT_h, wkT_h, wvT_h = hh["wqT"], hh["wkT"], hh["wvT"]
    bq_h, bk_h, bv_h = hh["bq"], hh["bk"], hh["bv_row"]
    onesc_h, onesr_h = hh["ones_col"], hh["ones_row"]
    qfm_h, kfm_h, vtm_h = hh["qfm"], hh["kfm"], hh["vtm"]
    marks = {}

    with ExitStack() as p1s:
        sb = lambda n, sh, dt: p1s.enter_context(nc.sbuf_tensor("s1_" + n, sh, dt))
        psf = lambda n, sh: p1s.enter_context(nc.psum_tensor("q1_" + n, sh, F32))

        xn = sb("xn", [128, NC * T], BF16)
        wq = sb("wq", [128, NC * D], BF16)
        wk = sb("wk", [128, NC * D], BF16)
        wv = sb("wv", [128, NC * D], BF16)
        bq = sb("bq", [128, NC], F32)
        bk = sb("bk", [128, NC], F32)
        bvb = sb("bvb", [128, D], F32)
        bv_r = sb("bvr", [1, D], F32)
        stage = sb("stage", [128, 4 * 512], BF16)
        vstage = sb("vstage", [128, 2 * D], BF16)
        C = ln_scratch(p1s, nc, sb, psf)
        C["ones_col"] = sb("onesc", [128, 1], BF16)
        C["ones_row"] = sb("onesr", [1, 128], F32)
        mm_ps = [psf("mm0", [128, 512]), psf("mm1", [128, 512])]
        v_ps = [psf("v0", [128, 1024]), psf("v1", [128, 1024])]

        ev_x = s.dma("sync", lambda h: h.dma_start(
            out=x[:, :].rearrange("p (c x) -> p c x", c=8),
            in_=x_h.ap().rearrange("(c p) t -> p c t", p=128)))
        ev_wq = s.dma("gpsimd", lambda h: h.dma_start(
            out=wq[:, :].rearrange("p (c x) -> p c x", c=8),
            in_=wqT_h.ap().rearrange("(c p) o -> p c o", p=128)))
        ev_wk = s.dma("gpsimd", lambda h: h.dma_start(
            out=wk[:, :].rearrange("p (c x) -> p c x", c=8),
            in_=wkT_h.ap().rearrange("(c p) o -> p c o", p=128)))
        ev_wv = s.dma("gpsimd", lambda h: h.dma_start(
            out=wv[:, :].rearrange("p (c x) -> p c x", c=8),
            in_=wvT_h.ap().rearrange("(c p) o -> p c o", p=128)))
        s.dma("sync", lambda h: h.dma_start(out=bq[:, :], in_=bq_h[:, :]))
        ev_b = s.dma("sync", lambda h: h.dma_start(out=bk[:, :], in_=bk_h[:, :]))
        ev_bv = s.dma("sync", lambda h: h.dma_start(out=bv_r[:, :], in_=bv_h[:, :]))
        s.dma("sync", lambda h: h.dma_start(out=C["ones_col"][:, :], in_=onesc_h[:, :]))
        ev_r = s.dma("sync", lambda h: h.dma_start(out=C["ones_row"][:, :], in_=onesr_h[:, :]))

        s.do("tensor", lambda h: h.matmul(v_ps[0][:, 0:512], C["ones_row"][0:1, :],
                                          bv_r[0:1, 0:512], start=True, stop=True),
             deps=[ev_bv, ev_r], mark=False)
        ev_bvb = s.do("tensor", lambda h: h.matmul(v_ps[0][:, 512:1024], C["ones_row"][0:1, :],
                                                   bv_r[0:1, 512:1024], start=True, stop=True))
        ev_bvb = s.do("scalar", lambda h: h.copy(bvb[:, :], v_ps[0][:, :]), deps=[ev_bvb])

        ln_evs, ln_prev = emit_ln_all(s, C, x, xn, ev_x)
        s.flush()

        prev_evac = [None, None]
        prev_store = [None, None, None, None]
        slot_box = [0]

        def proj_block(wsb, bias, out_h, wev, tag):
            for oc in range(NC):
                for qb in range(NQB):
                    slot = slot_box[0]
                    p = slot % 2
                    t0 = qb * QB
                    ev_mm = None
                    for dc in range(NC):
                        deps = []
                        if dc == 0:
                            deps = [ln_evs[qb], wev, prev_evac[p]]
                        ev_mm = s.do("tensor",
                                     (lambda wsb, oc, dc, t0, p: lambda h: h.matmul(
                                         mm_ps[p][:, :],
                                         wsb[:, dc * D + oc * 128: dc * D + oc * 128 + 128],
                                         xn[:, dc * T + t0: dc * T + t0 + QB],
                                         start=(dc == 0), stop=(dc == NC - 1)))(wsb, oc, dc, t0, p),
                                     deps=deps, mark=(dc == NC - 1))
                    ev_e = s.do("scalar",
                                (lambda p, oc, slot, bias: lambda h: h.activation(
                                    stage[:, (slot % 4) * 512:((slot % 4) + 1) * 512],
                                    mm_ps[p][:, :], AF.Identity, bias=bias[:, oc:oc + 1]))(p, oc, slot, bias),
                                deps=[ev_mm, ev_b, prev_store[slot % 4]])
                    prev_evac[p] = ev_e
                    ev_d = s.dma("sync",
                                 (lambda slot, oc, t0, out_h: lambda h: h.dma_start(
                                     out=out_h[oc * 128:(oc + 1) * 128, t0:t0 + QB],
                                     in_=stage[:, (slot % 4) * 512:((slot % 4) + 1) * 512]))(slot, oc, t0, out_h),
                                 deps=[ev_e])
                    prev_store[slot % 4] = ev_d
                    slot_box[0] += 1
            marks[tag] = (s.sem["dsync"], s.cnt["dsync"])

        proj_block(wk, bk, kfm_h, ev_wk, "k")
        s.flush()

        prev_add = [None, None]
        for tc in range(T // 128):
            p = tc % 2
            qb = (tc * 128) // QB
            ev_mm = None
            for half in range(2):
                for dc in range(NC):
                    deps = []
                    if half == 0 and dc == 0:
                        deps = [ln_evs[qb], ev_wv, prev_add[p], ev_bvb]
                    ev_mm = s.do("tensor",
                                 (lambda tc, dc, half, p: lambda h: h.matmul(
                                     v_ps[p][:, half * 512:(half + 1) * 512],
                                     xn[:, dc * T + tc * 128: dc * T + (tc + 1) * 128],
                                     wv[:, dc * D + half * 512: dc * D + (half + 1) * 512],
                                     start=(dc == 0), stop=(dc == NC - 1)))(tc, dc, half, p),
                                 deps=deps, mark=(dc == NC - 1 and half == 1))
            ev_a = s.do("vector",
                        (lambda p: lambda h: h.tensor_add(
                            vstage[:, p * D:(p + 1) * D], v_ps[p][:, :], bvb[:, :]))(p),
                        deps=[ev_mm])
            prev_add[p] = ev_a
            s.dma("sync",
                  (lambda tc, p: lambda h: h.dma_start(
                      out=vtm_h[tc * 128:(tc + 1) * 128, :],
                      in_=vstage[:, p * D:(p + 1) * D]))(tc, p),
                  deps=[ev_a])
        marks["v"] = (s.sem["dsync"], s.cnt["dsync"])
        s.flush()

        proj_block(wq, bq, qfm_h, ev_wq, "q")
        s.flush()
    return marks


def emit_kv_gather_dmas(s, hp, kag_h, vsrc, kside, vbuf, deps_k, deps_v):
    vag_h, vc0 = vsrc
    ev_k = None
    for r in range(4):
        for half in range(2):
            blk = r if half == 0 else 7 - r
            ev_k = s.dma("sync",
                         (lambda hp, r, half, blk: lambda h: h.dma_start(
                             out=kside[:, blk * 1024:(blk + 1) * 1024],
                             in_=kag_h[r * D + hp * 128: r * D + (hp + 1) * 128,
                                       half * 1024:(half + 1) * 1024]))(hp, r, half, blk),
                         deps=deps_k if (r == 0 and half == 0) else ())
    ev_v = None
    for r in range(4):
        for half in range(2):
            blk = r if half == 0 else 7 - r
            ev_v = s.dma("sync",
                         (lambda r, half, blk: lambda h: h.dma_start(
                             out=vbuf[:, blk * 1024:(blk + 1) * 1024]
                             .rearrange("p (k f) -> p k f", f=128),
                             in_=vag_h.ap()[r * T + half * 1024: r * T + (half + 1) * 1024,
                                            vc0: vc0 + 128]
                             .rearrange("(k p) f -> p k f", p=128)))(r, half, blk),
                         deps=deps_v if (r == 0 and half == 0) else ())
    return ev_k, ev_v


def emit_p2_body(nc, stack, s, hh, x, cc_k, cc_v, q_dep):
    qint_h, kag_h, vag_h = hh["qfm"], hh["kag"], hh["vag"]
    proj_h, masks_h = hh["proj"], hh["masks_all"]
    onesall_h, aug8_h = hh["ones_all"], hh["aug8"]

    with ExitStack() as p2s:
        sb = lambda n, sh, dt: p2s.enter_context(nc.sbuf_tensor("s2_" + n, sh, dt))
        zcat = sb("zcat", [128, HP * T], BF16)
        C = {}
        C["ones_all"] = sb("onesall", [128, 128], F32)
        aug8 = sb("aug8", [128, 1], BF16)
        C["recipb"] = sb("recipb", [128, 1024], F32)
        C["rb_sb"] = sb("rb", [128, 1024], BF16)

        s.dma("sync", lambda h: h.dma_start(out=C["ones_all"][:, :], in_=onesall_h[:, :]))
        ev_cst = s.dma("sync", lambda h: h.dma_start(out=aug8[:, :], in_=aug8_h[:, :]))

        with ExitStack() as ph:
            sbp = lambda n, sh, dt: ph.enter_context(nc.sbuf_tensor("sp2_" + n, sh, dt))
            psp = lambda n, sh: ph.enter_context(nc.psum_tensor("qp2_" + n, sh, F32))
            qown = sbp("qown", [128, HP * T], BF16)
            kside = sbp("kside", [128, S], BF16)
            vbuf = sbp("vbuf", [128, S], BF16)
            C["pbuf"] = sbp("pbuf", [128, 2048], BF16)
            C["mtile"] = sbp("mtile", [128, 3 * 512], BF16)
            C["sc0"] = psp("sc0", [128, 1024])
            C["sc1"] = psp("sc1", [128, 1024])
            C["pv"] = psp("pv", [128, 1024])
            C["smp"] = psp("smp", [128, 1024])

            ev_q = s.dma("sync", lambda h: h.dma_start(
                out=qown[:, :].rearrange("p (c x) -> p c x", c=8),
                in_=qint_h.ap().rearrange("(c p) t -> p c t", p=128)), deps=[q_dep])

            specs = causal_qb_specs()
            carry = {"qown_ev": ev_q}
            last = None
            for hp in range(HP):
                deps_k = [cc_k] + ([] if hp == 0 else [last["score"]])
                deps_v = [cc_v] + ([] if hp == 0 else [last["pv"]])
                ev_k, ev_v = emit_kv_gather_dmas(s, hp, kag_h, (vag_h, hp * 128),
                                                 kside, vbuf, deps_k, deps_v)
                last = emit_attention_hp(
                    s, C, hp, specs, 0.125, kside, qown, vbuf,
                    aug=lambda kt, hh2: aug8[:, 0:1], masks_h=masks_h, zcat=zcat,
                    ev_kv=(ev_k, ev_v, ev_cst), carry=carry)
            ev_attn_done = last["znlast"]
            s.flush()

        with ExitStack() as ph:
            sbp = lambda n, sh, dt: ph.enter_context(nc.sbuf_tensor("sp2_" + n, sh, dt))
            pw = sbp("pw", [128, NC * D], BF16)
            xstage = sbp("xstage", [128, 2 * 512], F32)
            psp = lambda n, sh: ph.enter_context(nc.psum_tensor("qp2_" + n, sh, F32))
            mm_ps = [psp("pm0", [128, 512]), psp("pm1", [128, 512])]

            ev_pw = s.dma("gpsimd", lambda h: h.dma_start(
                out=pw[:, :].rearrange("p (c x) -> p c x", c=8),
                in_=proj_h.ap().rearrange("(c p) o -> p c o", p=128)))
            emit_proj_residual(s, C, pw, zcat, x, None, mm_ps, xstage,
                               ev_attn_done, ev_pw, None)
            s.flush()


def emit_p4_body(nc, stack, s, hh, x, cc_q1, cc_q1t, cc_nb, q_dep):
    qint_h, kag_h, vag_h, nbag_h = hh["q1fm"], hh["q1ag"], hh["q1tag"], hh["nbag"]
    wkT_h, wvT_h, woT_h = hh["wk1T"], hh["wv1T"], hh["woT"]
    onesall_h, aug1_h = hh["ones_all"], hh["aug1"]

    with ExitStack() as p4s:
        sb = lambda n, sh, dt: p4s.enter_context(nc.sbuf_tensor("s4_" + n, sh, dt))
        zcat = sb("zcat", [128, HP * T], BF16)
        avcat = zcat
        mg = sb("mg", [128, HP * 64], BF16)
        C = {}
        C["ones_all"] = sb("onesall", [128, 128], F32)
        aug1 = sb("aug1", [128, 1], BF16)
        C["recipb"] = sb("recipb", [128, 1024], F32)
        C["rb_sb"] = sb("rb", [128, 1024], BF16)

        s.dma("sync", lambda h: h.dma_start(out=C["ones_all"][:, :], in_=onesall_h[:, :]))
        s.dma("sync", lambda h: h.dma_start(out=aug1[:, :], in_=aug1_h[:, :]))

        with ExitStack() as ph:
            sbp = lambda n, sh, dt: ph.enter_context(nc.sbuf_tensor("sp4_" + n, sh, dt))
            psp = lambda n, sh: ph.enter_context(nc.psum_tensor("qp4_" + n, sh, F32))
            wkv = sbp("wkv", [128, NC * D], BF16)
            wvv = sbp("wvv", [128, NC * D], BF16)
            mg_ps = psp("mgps", [128, 512])
            ev_wk = s.dma("gpsimd", lambda h: h.dma_start(
                out=wkv[:, :].rearrange("p (c x) -> p c x", c=8),
                in_=wkT_h.ap().rearrange("(c p) o -> p c o", p=128)))
            ev_wv = s.dma("gpsimd", lambda h: h.dma_start(
                out=wvv[:, :].rearrange("p (c x) -> p c x", c=8),
                in_=wvT_h.ap().rearrange("(c p) o -> p c o", p=128)))
            ev_mg = None
            for hp in range(HP):
                for hhi in range(2):
                    hd = hp * 128 + hhi * 64
                    for dc in range(NC):
                        ev_mg = s.do("tensor",
                                     (lambda hp, hhi, hd, dc: lambda h: h.matmul(
                                         mg_ps[hhi * 64:(hhi + 1) * 64, hp * 64:(hp + 1) * 64],
                                         wkv[:, dc * D + hd: dc * D + hd + 64],
                                         wvv[:, dc * D + hd: dc * D + hd + 64],
                                         start=(dc == 0), stop=(dc == NC - 1),
                                         tile_position=(0, 64 * hhi)))(hp, hhi, hd, dc),
                                     deps=[ev_wk, ev_wv] if (hp == 0 and hhi == 0 and dc == 0) else (),
                                     mark=(dc == NC - 1))
            ev_mge = s.do("scalar", lambda h: h.copy(mg[:, :], mg_ps[:, 0:512]),
                          deps=[ev_mg])
            s.flush()

        with ExitStack() as ph:
            sbp = lambda n, sh, dt: ph.enter_context(nc.sbuf_tensor("sp4_" + n, sh, dt))
            psp = lambda n, sh: ph.enter_context(nc.psum_tensor("qp4_" + n, sh, F32))
            qown = sbp("qown", [128, HP * T], BF16)
            kside = sbp("kside", [128, S], BF16)
            vbuf = sbp("vbuf", [128, S], BF16)
            esqs = sbp("esqs", [128, 128], F32)
            C["pbuf"] = sbp("pbuf", [128, 2048], BF16)
            C["sc0"] = psp("sc0", [128, 1024])
            C["sc1"] = psp("sc1", [128, 1024])
            C["pv"] = psp("pv", [128, 1024])
            C["smp"] = psp("smp", [128, 1024])

            ev_q = s.dma("sync", lambda h: h.dma_start(
                out=qown[:, :].rearrange("p (c x) -> p c x", c=8),
                in_=qint_h.ap().rearrange("(c p) t -> p c t", p=128)), deps=[q_dep])

            carry = {"qown_ev": ev_q}
            last = None
            for hp in range(HP):
                deps_k = [cc_q1] + ([] if hp == 0 else [last["score"]])
                deps_v = [cc_q1t[hp // 2]] + ([] if hp == 0 else [last["pv"]])
                deps_e = [cc_nb] + ([] if hp == 0 else [last["sum"]])
                ev_k, ev_v = emit_kv_gather_dmas(
                    s, hp, kag_h, (vag_h[hp // 2], (hp % 2) * 128),
                    kside, vbuf, deps_k, deps_v)
                ev_e = None
                for r in range(4):
                    for half in range(2):
                        blk = r if half == 0 else 7 - r
                        ev_e = s.dma("gpsimd",
                                     (lambda hp, r, half, blk: lambda h: h.dma_start(
                                         out=esqs[:, blk * 16:(blk + 1) * 16]
                                         .rearrange("p (k e) -> p k e", e=2),
                                         in_=nbag_h.ap()[r * T + half * 1024:
                                                         r * T + (half + 1) * 1024,
                                                         hp * 2: hp * 2 + 2]
                                         .rearrange("(k p) e -> p k e", p=128)))(hp, r, half, blk),
                                     deps=deps_e if (r == 0 and half == 0) else ())
                last = emit_attention_hp(
                    s, C, hp, [(qb * QB, [(g, None) for g in range(64)])
                               for qb in range(NQB)],
                    0.25, kside, qown, vbuf,
                    aug=lambda kt, hh2: aug1[:, 0:1],
                    masks_h=None, zcat=zcat, ev_kv=(ev_k, ev_v, ev_e), carry=carry,
                    biasf=lambda kt, hh2: esqs[:, kt * 2 + hh2: kt * 2 + hh2 + 1])
            ev_attn_done = last["znlast"]
            s.flush()

        with ExitStack() as ph:
            psp = lambda n, sh: ph.enter_context(nc.psum_tensor("qp4_" + n, sh, F32))
            av_ps = [psp("av0", [128, 512]), psp("av1", [128, 512])]
            prev_av = [None, None]
            ev_av_all = None
            for hp in range(HP):
                for qb in range(NQB):
                    t0 = qb * QB
                    p = (hp * NQB + qb) % 2
                    deps0 = [ev_mge, prev_av[p]]
                    if hp == 0 and qb == 0:
                        deps0.append(ev_attn_done)
                    s.do("tensor",
                         (lambda hp, t0, p: lambda h: h.matmul(
                             av_ps[p][0:64, :], mg[0:64, hp * 64:(hp + 1) * 64],
                             zcat[0:64, hp * T + t0: hp * T + t0 + QB],
                             start=True, stop=True, tile_position=(0, 0)))(hp, t0, p),
                         deps=deps0, mark=False)
                    ev_m2 = s.do("tensor",
                                 (lambda hp, t0, p: lambda h: h.matmul(
                                     av_ps[p][64:128, :], mg[64:128, hp * 64:(hp + 1) * 64],
                                     zcat[64:128, hp * T + t0: hp * T + t0 + QB],
                                     start=True, stop=True, tile_position=(64, 64)))(hp, t0, p))
                    ev_ae = s.do("scalar",
                                 (lambda hp, t0, p: lambda h: h.activation(
                                     avcat[:, hp * T + t0: hp * T + t0 + QB], av_ps[p][:, :],
                                     AF.Copy, scale=0.125))(hp, t0, p),
                                 deps=[ev_m2])
                    prev_av[p] = ev_ae
                    ev_av_all = ev_ae
            s.flush()

        with ExitStack() as ph:
            sbp = lambda n, sh, dt: ph.enter_context(nc.sbuf_tensor("sp4_" + n, sh, dt))
            psp = lambda n, sh: ph.enter_context(nc.psum_tensor("qp4_" + n, sh, F32))
            pw = sbp("pw", [128, NC * D], BF16)
            xstage = sbp("xstage", [128, 2 * 512], F32)
            mm_ps = [psp("pm0", [128, 512]), psp("pm1", [128, 512])]
            ev_pw = s.dma("gpsimd", lambda h: h.dma_start(
                out=pw[:, :].rearrange("p (c x) -> p c x", c=8),
                in_=woT_h.ap().rearrange("(c p) o -> p c o", p=128)))
            emit_proj_residual(s, C, pw, avcat, x, None, mm_ps, xstage,
                               ev_av_all, ev_pw, None)
            s.flush()


def build_mega():
    nc = bass.Bass("TRN2", target_bir_lowering=False)
    hh = {}
    ins = (("x", [D, T], F32),
           ("wqT", [D, D], F32), ("wkT", [D, D], F32), ("wvT", [D, D], F32),
           ("bq", [128, NC], F32), ("bk", [128, NC], F32), ("bv_row", [1, D], F32),
           ("proj", [D, D], F32), ("masks_all", [NMASK, 128, 512], BF16),
           ("ones_all", [128, 128], F32), ("aug8", [128, 1], BF16),
           ("aug1", [128, 1], BF16), ("ones_col", [128, 1], BF16),
           ("ones_row", [1, 128], F32), ("ones_row_bf", [1, 128], BF16),
           ("ident_bf", [128, 128], BF16),
           ("wT", [D, D], F32), ("wpT", [D, 256], F32), ("bp", [128, 2], F32),
           ("sim", [256, 4], F32), ("temp", [1, 1], F32), ("wdT", [D, 15], F32),
           ("wuT", [15, D], F32), ("segmat4", [4, 15], BF16),
           ("wk1T", [D, D], F32), ("wv1T", [D, D], F32), ("woT", [D, D], F32),
           ("wr1", [D, 256], F32), ("br1", [128, 2], F32), ("wr2", [256, 8], F32),
           ("br2_row", [1, 8], F32), ("ws1", [D, 4096], F32), ("bs1", [128, 32], F32),
           ("ws2", [4096, D], F32), ("bs2", [128, NC], F32),
           ("we1", [8, D, 4096], F32), ("be1", [8, 128, 32], F32),
           ("we2", [8, 4096, D], F32), ("be2_row", [1, 8 * D], F32))
    for nm, shp, dt in ins:
        hh[nm] = nc.dram_tensor(nm, shp, dt, kind="ExternalInput")
    hh["out"] = nc.dram_tensor("out", [D, T], F32, kind="ExternalOutput")
    hh["qfm"] = nc.dram_tensor("i_qfm", [D, T], BF16)
    hh["kfm"] = nc.dram_tensor("i_kfm", [D, T], BF16)
    hh["vtm"] = nc.dram_tensor("i_vtm", [T, D], BF16)
    hh["kag"] = nc.dram_tensor("i_kag", [4 * D, T], BF16)
    hh["vag"] = nc.dram_tensor("i_vag", [4 * T, D], BF16)
    hh["q1fm"] = nc.dram_tensor("i_q1fm", [D, T], BF16)
    hh["q1tm"] = [nc.dram_tensor(f"i_q1tm{c}", [T, 256], BF16) for c in range(4)]
    hh["nb"] = nc.dram_tensor("i_nb", [T, H], F32)
    hh["q1ag"] = nc.dram_tensor("i_q1ag", [4 * D, T], BF16)
    hh["q1tag"] = [nc.dram_tensor(f"i_q1tag{c}", [4 * T, 256], BF16) for c in range(4)]
    hh["nbag"] = nc.dram_tensor("i_nbag", [4 * T, H], F32)
    hh["x2"] = nc.dram_tensor("i_x2", [D, T], F32)

    with ExitStack() as stack:
        s = Sch(nc, stack)
        ccsems = [stack.enter_context(nc.semaphore(f"cc{i}")) for i in range(8)]
        x = stack.enter_context(nc.sbuf_tensor("s_x", [128, NC * T], F32))

        def mkcc(i, src, dst, dep):
            s.do("gpsimd",
                 (lambda i, src, dst: lambda h: h.collective_compute(
                     "AllGather", ALU.bypass, replica_groups=RG4,
                     ins=[src.ap().opt()], outs=[dst.ap().opt()])
                  .then_inc(ccsems[i]))(i, src, dst),
                 deps=[dep], mark=False)
            return (ccsems[i], 1)

        marks = emit_p1_body(nc, stack, s, hh, x)
        cc_k = mkcc(0, hh["kfm"], hh["kag"], marks["k"])
        cc_v = mkcc(1, hh["vtm"], hh["vag"], marks["v"])
        s.barrier()
        s.flush()
        emit_p2_body(nc, stack, s, hh, x, cc_k, cc_v, marks["q"])
        s.barrier()
        s.flush()
        with ExitStack() as p3s:
            m3 = emit_p3_body(nc, p3s, s, hh, x=x)
            s.flush()
        d3 = (s.sem["dsync"], s.cnt["dsync"])
        cc_q1 = mkcc(2, hh["q1fm"], hh["q1ag"], m3["q1fm"])
        cc_nb = mkcc(3, hh["nb"], hh["nbag"], d3)
        cc_q1t = [mkcc(4 + c, hh["q1tm"][c], hh["q1tag"][c], d3) for c in range(4)]
        s.barrier()
        s.flush()
        emit_p4_body(nc, stack, s, hh, x, cc_q1, cc_q1t, cc_nb, m3["q1fm"])
        s.barrier()
        s.flush()
        hh5 = dict(hh)
        hh5["ones_col5"] = hh["ones_col"]
        hh5["ones_row5"] = hh["ones_row"]
        hh5["ident_bf5"] = hh["ident_bf"]
        hh5.pop("x")
        emit_p5_body(nc, stack, s, hh5, x=x)
        s.final_wait()
    return nc



def _f(x):
    return np.ascontiguousarray(np.asarray(x), dtype=np.float32)


def chunk_bias(b):
    return np.ascontiguousarray(b.reshape(NC, 128).T, dtype=np.float32)


def make_host_consts():
    c = {}
    c["ident_bf"] = np.eye(128, dtype=BF)
    c["ones_col"] = np.ones((128, 1), dtype=BF)
    c["ones_row"] = np.ones((1, 128), dtype=np.float32)
    c["ones_row_bf"] = np.ones((1, 128), dtype=BF)
    c["aug8"] = np.full((128, 1), 8.0, dtype=BF)
    c["aug1"] = np.ones((128, 1), dtype=BF)
    c["ones_all"] = np.ones((128, 128), dtype=np.float32)
    seg = np.zeros((4, 15), dtype=BF)
    seg[0, 0] = 1
    seg[1, 1:3] = 1
    seg[2, 3:7] = 1
    seg[3, 7:15] = 1
    c["segmat4"] = seg
    return c


def prep_weights(inp):
    w = {}
    wq = _f(inp["L0_attn_Wq"]).reshape(H * DH, D)
    wk = _f(inp["L0_attn_Wk"]).reshape(H * DH, D)
    wv = _f(inp["L0_attn_Wv"]).reshape(H * DH, D)
    w["l0_wqT"] = np.ascontiguousarray(wq.T)
    w["l0_wkT"] = np.ascontiguousarray(wk.T)
    w["l0_wvT"] = np.ascontiguousarray(wv.T)
    w["l0_bq"] = chunk_bias(_f(inp["L0_attn_bq"]).reshape(-1))
    w["l0_bk"] = chunk_bias(_f(inp["L0_attn_bk"]).reshape(-1))
    w["l0_bv_row"] = _f(inp["L0_attn_bv"]).reshape(1, H * DH)
    w["l0_proj"] = _f(inp["L0_attn_proj"])
    w["l0_wT"] = np.ascontiguousarray(_f(inp["L0_ffn_W"]).T)
    w["l0_wpT"] = np.ascontiguousarray(_f(inp["L0_ffn_Wp"]).T)
    w["l0_bp"] = np.ascontiguousarray(_f(inp["L0_ffn_bp"]).reshape(2, 128).T)
    w["l0_sim"] = _f(inp["L0_ffn_sim"])
    w["l0_temp"] = _f(inp["L0_ffn_temp"]).reshape(1, 1)
    wd = np.concatenate([_f(inp[f"L0_ffn_Wd{i}"]) for i in range(4)], axis=0)
    wu = np.concatenate([_f(inp[f"L0_ffn_Wu{i}"]) for i in range(4)], axis=1)
    w["l0_wdT"] = np.ascontiguousarray(wd.T)
    w["l0_wuT"] = np.ascontiguousarray(wu.T)
    w["l1_wkT"] = np.ascontiguousarray(_f(inp["L1_attn_W_k"]).T)
    w["l1_wvT"] = np.ascontiguousarray(_f(inp["L1_attn_W_v"]).T)
    w["l1_woT"] = np.ascontiguousarray(_f(inp["L1_attn_W_o"]).T)
    w["l1_wr1"] = _f(inp["L1_ffn_Wr1"])
    w["l1_br1"] = np.ascontiguousarray(_f(inp["L1_ffn_br1"]).reshape(2, 128).T)
    w["l1_wr2"] = _f(inp["L1_ffn_Wr2"])
    w["l1_br2_row"] = _f(inp["L1_ffn_br2"]).reshape(1, 8)
    w["l1_ws1"] = _f(inp["L1_ffn_Ws1"])
    w["l1_bs1"] = np.ascontiguousarray(_f(inp["L1_ffn_bs1"]).reshape(32, 128).T)
    w["l1_ws2"] = _f(inp["L1_ffn_Ws2"])
    w["l1_bs2"] = chunk_bias(_f(inp["L1_ffn_bs2"]).reshape(-1))
    w["l1_we1"] = _f(inp["L1_ffn_We1"])
    w["l1_be1"] = np.ascontiguousarray(
        _f(inp["L1_ffn_be1"]).reshape(8, 32, 128).transpose(0, 2, 1))
    w["l1_we2"] = _f(inp["L1_ffn_We2"])
    w["l1_be2_row"] = np.ascontiguousarray(_f(inp["L1_ffn_be2"]).reshape(1, 8 * D))
    return w


def shard_tokens(core):
    return core // 4, core % 4, 7 - (core % 4)


def build_core_masks(j):
    qs = [j * 1024, j * 1024 + 512, (7 - j) * 1024, (7 - j) * 1024 + 512]
    kk = np.arange(128)[:, None]
    qq = np.arange(512)[None, :]
    tiles = []
    for sidx in range(NQB):
        Qs = qs[sidx]
        for g in range(CAUSAL_G0[sidx], CAUSAL_B[sidx]):
            if g * 128 + 127 <= Qs:
                tiles.append(np.ones((128, 512), dtype=BF))
            elif g * 128 > Qs + 511:
                tiles.append(np.zeros((128, 512), dtype=BF))
            else:
                tiles.append((Qs + qq >= g * 128 + kk).astype(BF))
    assert len(tiles) == NMASK
    return np.ascontiguousarray(np.stack(tiles))


def make_vall(vfull):
    v = np.asarray(vfull).reshape(64, 128, HP, 128)
    return np.ascontiguousarray(v.transpose(2, 1, 0, 3).reshape(HP, 128, S))


def make_esqall(esqfull):
    e = np.asarray(esqfull).reshape(64, 128, HP, 2)
    return np.ascontiguousarray(e.transpose(2, 1, 0, 3).reshape(HP, 128, 128))


def gather_full(slices, cores, part=None):
    out = None
    for core in cores:
        b, b0, b1 = shard_tokens(core)
        xc = np.asarray(slices[core])
        if out is None:
            out = np.zeros((S,) + xc.shape[1:], dtype=xc.dtype)
        out[b0 * 1024:(b0 + 1) * 1024] = xc[0:1024]
        out[b1 * 1024:(b1 + 1) * 1024] = xc[1024:2048]
    return out


def gather_full_fm(slices, cores):
    out = None
    for core in cores:
        b, b0, b1 = shard_tokens(core)
        xc = np.asarray(slices[core])
        if out is None:
            out = np.zeros((xc.shape[0], S), dtype=xc.dtype)
        out[:, b0 * 1024:(b0 + 1) * 1024] = xc[:, 0:1024]
        out[:, b1 * 1024:(b1 + 1) * 1024] = xc[:, 1024:2048]
    return out



def kernel(**inp):
    consts = make_host_consts()
    w = prep_weights(inp)
    x_full = _f(inp["L0_attn_x"])

    x_slices = []
    for core in range(N_CORES):
        b, b0, b1 = shard_tokens(core)
        xs = np.concatenate([x_full[b, b0 * 1024:(b0 + 1) * 1024, :],
                             x_full[b, b1 * 1024:(b1 + 1) * 1024, :]], axis=0)
        x_slices.append(np.ascontiguousarray(xs.T))
    core_masks = [build_core_masks(c % 4) for c in range(N_CORES)]

    nc = build_mega()
    im = [{"x": x_slices[c],
           "wqT": w["l0_wqT"], "wkT": w["l0_wkT"], "wvT": w["l0_wvT"],
           "bq": w["l0_bq"], "bk": w["l0_bk"], "bv_row": w["l0_bv_row"],
           "proj": w["l0_proj"], "masks_all": core_masks[c],
           "ones_all": consts["ones_all"], "aug8": consts["aug8"],
           "aug1": consts["aug1"], "ones_col": consts["ones_col"],
           "ones_row": consts["ones_row"], "ones_row_bf": consts["ones_row_bf"],
           "ident_bf": consts["ident_bf"],
           "wT": w["l0_wT"], "wpT": w["l0_wpT"], "bp": w["l0_bp"],
           "sim": w["l0_sim"], "temp": w["l0_temp"], "wdT": w["l0_wdT"],
           "wuT": w["l0_wuT"], "segmat4": consts["segmat4"],
           "wk1T": w["l1_wkT"], "wv1T": w["l1_wvT"], "woT": w["l1_woT"],
           "wr1": w["l1_wr1"], "br1": w["l1_br1"], "wr2": w["l1_wr2"],
           "br2_row": w["l1_br2_row"], "ws1": w["l1_ws1"], "bs1": w["l1_bs1"],
           "ws2": w["l1_ws2"], "bs2": w["l1_bs2"], "we1": w["l1_we1"],
           "be1": w["l1_be1"], "we2": w["l1_we2"], "be2_row": w["l1_be2_row"]}
          for c in range(N_CORES)]
    r = run_bass_kernel_spmd(nc, im, core_ids=list(range(N_CORES))).results

    out = np.zeros((B, S, D), dtype=np.float32)
    for core in range(N_CORES):
        b, b0, b1 = shard_tokens(core)
        xc = np.asarray(r[core]["out"], dtype=np.float32).T
        out[b, b0 * 1024:(b0 + 1) * 1024, :] = xc[0:1024, :]
        out[b, b1 * 1024:(b1 + 1) * 1024, :] = xc[1024:2048, :]
    return out
